# version 23
# baseline (speedup 1.0000x reference)
import numpy as np
import ml_dtypes
from contextlib import ExitStack

import concourse.bass as bass
import concourse.mybir as mybir
from concourse.bass_utils import run_bass_kernel_spmd
from concourse.aot_env import allow_ici_all_to_all

F32 = mybir.dt.float32
BF16 = mybir.dt.bfloat16
I32 = mybir.dt.int32
AF = mybir.ActivationFunctionType
ALU = mybir.AluOpType
AX = mybir.AxisListType
BF = ml_dtypes.bfloat16

DM, DI, DXZ, HF, DS, DC, DR = 1024, 2048, 4096, 3072, 16, 4, 64
NCORES, SEQ = 8, 16384
S_IN, S_XZ, S_CONV, S_XDBL, S_DT, S_Y = 0.05, 0.06, 0.04, 0.05, 0.05, 0.1
CM = float(1.5 * 2 ** 23)
EPS = 1e-5
NB_DM, NB_DI, NB_XZ, NB_HF = DM // 128, DI // 128, DXZ // 128, HF // 128
HALO = 4
NT_OWN = SEQ // NCORES
SC_TC = 512
DEBUG_L1 = False
USE_FUSED = True


class Sem:
    def __init__(self, handle, name):
        self.h, self.name, self.count = handle, name, 0


class Prog:
    ENGS = ("sync", "scalar", "vector", "tensor", "gpsimd")

    def __init__(self, nc, esems, last_wait):
        self.nc = nc
        self.streams = {e: [] for e in self.ENGS}
        self.es = esems
        self._lw = last_wait

    def _wl(self, eng, reads):
        wl = []
        for w in reads:
            if w is None:
                continue
            s, cnt = (w, w.count) if isinstance(w, Sem) else w
            if cnt is None or cnt <= 0:
                continue
            key = (eng, s.name)
            if self._lw.get(key, -1) >= cnt:
                continue
            self._lw[key] = cnt
            wl.append((s.h, cnt))
        return wl

    def op(self, eng, build, reads=(), dma=None):
        wl = self._wl(eng, reads)
        sem, n = (dma, 16) if dma is not None else (self.es[eng], 1)
        sem.count += n

        def run(e, wl=wl, build=build, sem=sem, n=n):
            for h, cnt in wl:
                e.wait_ge(h, cnt)
            build(e).then_inc(sem.h, n)
        self.streams[eng].append(run)
        return (sem, sem.count)

    def waits(self, eng, reads):
        wl = self._wl(eng, reads)

        def run(e, wl=wl):
            for h, cnt in wl:
                e.wait_ge(h, cnt)
        self.streams[eng].append(run)

    def drain(self, sems):
        lst = list(sems)
        for eng in ("sync", "scalar", "gpsimd"):
            self.waits(eng, lst)

    def realize(self, block):
        for name in self.ENGS:
            stream = self.streams[name]
            if stream:
                getattr(block, name)(lambda e, s=stream: [fn(e) for fn in s])


class Ring:
    def __init__(self, n, sems):
        self.n, self.sems = n, sems
        self.free_at = [None] * n
        self.i = 0

    def acquire(self):
        j = self.i % self.n
        self.i += 1
        return j, self.free_at[j]

    def release(self, j, semcount):
        self.free_at[j] = semcount


def subtiles(total, first=None):
    out, off = [], 0
    if first:
        out.append((0, first))
        off = first
    while off < total:
        w = min(512, total - off)
        out.append((off, w))
        off += w
    return out


def split_supers(subs, maxw=1028):
    sup, cur, wsum = [], [], 0
    for (off, w) in subs:
        if cur and wsum + w > maxw:
            sup.append(cur)
            cur, wsum = [], 0
        cur.append((off, w))
        wsum += w
    if cur:
        sup.append(cur)
    return sup


class Ctx:
    pass


class D_:
    pass


def emit_fq3(P, c, dst_ap, src_ap, w, scale, extra_reads=(), parts=128):
    P.op("scalar", lambda e: e.activation(
        c.stage[0:parts, 0:w], src_ap, AF.Identity,
        bias=c.cmag[0:parts, 0:1], scale=scale),
        reads=list(extra_reads) + [c.te, c.sc, c.ve])
    rel_src = (c.sc, P.es["scalar"].count)
    P.op("vector", lambda e: e.tensor_scalar(
        c.stage[0:parts, 512:512 + w], c.stage[0:parts, 0:w], CM, 127.0,
        ALU.subtract, ALU.min), reads=[c.sc, c.ve])
    P.op("vector", lambda e: e.tensor_scalar(
        dst_ap, c.stage[0:parts, 512:512 + w], -128.0, None, ALU.max),
        reads=[c.ve])
    return rel_src


def emit_ln(P, c, subs, apply_fn, fq_scale=None):
    R = c.lnrow
    for (off, w) in subs:
        for b in range(NB_DM):
            P.op("tensor", lambda e, b=b, off=off, w=w: e.matmul(
                c.p_sum[0:1, 0:w], c.ones_c[:], c.resid[:, b, off:off + w],
                start=(b == 0), stop=(b == NB_DM - 1)),
                reads=[c.resid_ready, c.q1, c.rel_ps])
        for b in range(NB_DM):
            js = b % 2
            sqr = P.op("scalar", lambda e, b=b, js=js, off=off, w=w: e.square(
                c.sq8[:, js, 0:w], c.resid[:, b, off:off + w]),
                reads=[c.resid_ready, c.rel_sq2[js], c.sc])
            P.op("tensor", lambda e, b=b, js=js, w=w: e.matmul(
                c.p_sq[0:1, 0:w], c.ones_c[:], c.sq8[:, js, 0:w],
                start=(b == 0), stop=(b == NB_DM - 1)), reads=[sqr])
            c.rel_sq2[js] = (c.te, P.es["tensor"].count)
        c.rel_sq = (c.te, P.es["tensor"].count)
        P.op("vector", lambda e, w=w: e.tensor_scalar(
            R[0:1, 0:w], c.p_sum[0:1, 0:w], 1.0 / DM, None, ALU.mult),
            reads=[c.te, c.ve])
        P.op("vector", lambda e, w=w: e.tensor_scalar(
            R[0:1, 512:512 + w], c.p_sq[0:1, 0:w], 1.0 / DM, None, ALU.mult),
            reads=[c.ve])
        c.rel_ps = (c.ve, P.es["vector"].count)
        P.op("vector", lambda e, w=w: e.tensor_tensor(
            R[0:1, 1024:1024 + w], R[0:1, 0:w], R[0:1, 0:w], ALU.mult),
            reads=[c.ve])
        P.op("vector", lambda e, w=w: e.tensor_tensor(
            R[0:1, 1024:1024 + w], R[0:1, 512:512 + w], R[0:1, 1024:1024 + w],
            ALU.subtract), reads=[c.ve])
        P.op("scalar", lambda e, w=w: e.activation(
            R[0:1, 1536:1536 + w], R[0:1, 1024:1024 + w], AF.Ln,
            bias=c.eps[0:1, 0:1]), reads=[c.ve, c.sc])
        P.op("scalar", lambda e, w=w: e.activation(
            R[0:1, 1536:1536 + w], R[0:1, 1536:1536 + w], AF.Exp,
            scale=-0.5), reads=[c.sc])
        if fq_scale is not None:
            P.op("vector", lambda e, w=w: e.tensor_scalar(
                R[0:1, 1536:1536 + w], R[0:1, 1536:1536 + w], float(fq_scale),
                None, ALU.mult), reads=[c.sc, c.ve])
        P.op("vector", lambda e, w=w: e.tensor_tensor(
            R[0:1, 1024:1024 + w], R[0:1, 0:w], R[0:1, 1536:1536 + w],
            ALU.mult), reads=[c.sc, c.ve])
        P.op("tensor", lambda e, w=w: e.matmul(
            c.p_bc1[:, 0:w], c.ones_r[:], R[0:1, 1536:1536 + w],
            start=True, stop=True), reads=[c.ve, c.sc, c.q1, c.rel_bc])
        P.op("tensor", lambda e, w=w: e.matmul(
            c.p_bc2[:, 0:w], c.ones_r[:], R[0:1, 1024:1024 + w],
            start=True, stop=True), reads=[c.ve])
        for b in range(NB_DM):
            apply_fn(b, off, w)
        c.rel_bc = (c.ve, P.es["vector"].count)


def emit_ffn(P, c, supers, wgq, wuq, wdq, steps=()):
    steps = list(steps)
    tick = lambda: steps and steps.pop(0)(P)

    for sup in supers:
        off0 = sup[0][0]

        def apply(b, off, w, off0=off0):
            P.op("vector", lambda e, b=b, off=off, w=w: e.tensor_tensor(
                c.stage[:, 1024:1024 + w], c.resid[:, b, off:off + w],
                c.p_bc1[:, 0:w], ALU.mult), reads=[c.te, c.ve])
            P.op("vector", lambda e, b=b, off=off, w=w, off0=off0: e.tensor_tensor(
                c.xln[:, b, off - off0:off - off0 + w],
                c.stage[:, 1024:1024 + w], c.p_bc2[:, 0:w], ALU.subtract),
                reads=[c.ve])
        emit_ln(P, c, sup, apply)
        xln_done = (c.ve, P.es["vector"].count)
        tick()

        rel_pg2 = [c.rel_pg, c.rel_pg]
        rel_pu2 = [c.rel_pu, c.rel_pu]
        sstage = [2048, 3072]
        rel_ss = [None, None]
        for hc in range(NB_HF):
            jp = hc % 2
            ja, relA = c.rA.acquire()
            P.op("sync", lambda e, hc=hc, ja=ja: e.dma_start(
                c.wbufA[:, ja, :], wgq[hc]),
                reads=[relA, getattr(c, "m_wg", None)], dma=c.rA.sems[ja])
            ldA = (c.rA.sems[ja], c.rA.sems[ja].count)
            jb, relB = c.rB.acquire()
            P.op("scalar", lambda e, hc=hc, jb=jb: e.dma_start(
                c.wbufB[:, jb, :], wuq[hc]),
                reads=[relB, getattr(c, "m_wu", None)], dma=c.rB.sems[jb])
            ldB = (c.rB.sems[jb], c.rB.sems[jb].count)
            for (off, w) in sup:
                o2 = off - off0
                for dc in range(NB_DM):
                    P.op("tensor", lambda e, dc=dc, o2=o2, w=w, ja=ja, jp=jp: e.matmul(
                        c.p_g[:, jp, 0:w], c.wbufA[:, ja, dc * 128:(dc + 1) * 128],
                        c.xln[:, dc, o2:o2 + w], start=(dc == 0),
                        stop=(dc == NB_DM - 1)), reads=[ldA, xln_done, rel_pg2[jp]])
                for dc in range(NB_DM):
                    P.op("tensor", lambda e, dc=dc, o2=o2, w=w, jb=jb, jp=jp: e.matmul(
                        c.p_u[:, jp, 0:w], c.wbufB[:, jb, dc * 128:(dc + 1) * 128],
                        c.xln[:, dc, o2:o2 + w], start=(dc == 0),
                        stop=(dc == NB_DM - 1)), reads=[ldB, rel_pu2[jp]])
                mm_done = (c.te, P.es["tensor"].count)
                P.op("scalar", lambda e, w=w, jp=jp: e.activation(
                    c.stage[:, sstage[jp]:sstage[jp] + w], c.p_g[:, jp, 0:w],
                    AF.Silu, scale=1.0 / 127.0, bias=c.zero[:, 0:1]),
                    reads=[mm_done, c.sc, rel_ss[jp]])
                rel_pg2[jp] = (c.sc, P.es["scalar"].count)
                P.op("vector", lambda e, hc=hc, o2=o2, w=w, jp=jp: e.tensor_tensor(
                    c.h_ffn[:, hc, o2:o2 + w], c.stage[:, sstage[jp]:sstage[jp] + w],
                    c.p_u[:, jp, 0:w], ALU.mult), reads=[rel_pg2[jp], mm_done, c.ve])
                rel_pu2[jp] = (c.ve, P.es["vector"].count)
                rel_ss[jp] = rel_pu2[jp]
            c.rA.release(ja, (c.te, P.es["tensor"].count))
            c.rB.release(jb, (c.te, P.es["tensor"].count))
            tick()
        c.rel_pg = rel_pg2[1]
        c.rel_pu = rel_pu2[1]
        h_done = (c.ve, P.es["vector"].count)

        sdl = [512, 1536]
        for oc in range(NB_DM):
            jd, relD = c.rD.acquire()
            P.op("sync" if oc % 2 == 0 else "scalar",
                 lambda e, oc=oc, jd=jd: e.dma_start(
                     c.wdbuf[:, jd, :], wdq[oc]),
                 reads=[relD, getattr(c, "m_wd", None)], dma=c.rD.sems[jd])
            ldD = (c.rD.sems[jd], c.rD.sems[jd].count)
            for (off, w) in sup:
                o2 = off - off0
                jp = c.jd2
                for hc in range(NB_HF):
                    P.op("tensor", lambda e, hc=hc, o2=o2, w=w, jd=jd, jp=jp: e.matmul(
                        c.p_d[:, jp, 0:w], c.wdbuf[:, jd, hc * 128:(hc + 1) * 128],
                        c.h_ffn[:, hc, o2:o2 + w], start=(hc == 0),
                        stop=(hc == NB_HF - 1)), reads=[ldD, h_done, c.rel_pd2[jp]])
                P.op("scalar", lambda e, w=w, jp=jp: e.activation(
                    c.stage[:, sdl[jp]:sdl[jp] + w], c.p_d[:, jp, 0:w],
                    AF.Identity, scale=0.5 / (127.0 * 127.0),
                    bias=c.zero[:, 0:1]), reads=[c.te, c.sc, c.rel_sd2[jp]])
                P.op("vector", lambda e, oc=oc, off=off, w=w, jp=jp: e.tensor_tensor(
                    c.resid[:, oc, off:off + w], c.resid[:, oc, off:off + w],
                    c.stage[:, sdl[jp]:sdl[jp] + w], ALU.add),
                    reads=[c.sc, c.ve])
                c.rel_pd2[jp] = (c.ve, P.es["vector"].count)
                c.rel_sd2[jp] = c.rel_pd2[jp]
                c.jd2 = 1 - jp
            c.rD.release(jd, (c.te, P.es["tensor"].count))
            tick()
    while steps:
        steps.pop(0)(P)
    c.resid_ready = (c.ve, P.es["vector"].count)


def quant_weight(P, c, raw_ap, n, recip_col, codes_dst, extra=()):
    P.op("scalar", lambda e: e.activation(
        c.stage[:, 0:n], raw_ap, AF.Identity, bias=c.cmag[:, 0:1],
        scale=recip_col), reads=list(extra) + [c.sc, c.ve, c.q1])
    P.op("vector", lambda e: e.tensor_scalar(
        c.stage[:, 2048:2048 + n], c.stage[:, 0:n], CM, 127.0,
        ALU.subtract, ALU.min), reads=[c.sc, c.ve])
    P.op("vector", lambda e: e.tensor_scalar(
        codes_dst, c.stage[:, 2048:2048 + n], -128.0, None, ALU.max),
        reads=[c.ve])


def emit_colscale(P, c, row_src_ap, dst_col, pslot):
    P.op("gpsimd", lambda e: e.dma_start(
        c.d_cs[0:1, dst_col:dst_col + 1], row_src_ap),
        reads=[c.ve], dma=c.qg2)
    mid = (c.qg2, c.qg2.count)
    P.op("gpsimd", lambda e: e.dma_start(
        c.acol[:, dst_col:dst_col + 1],
        c.d_cs[0:1, dst_col:dst_col + 1].broadcast_to([128, 1])),
        reads=[mid], dma=c.qg2)
    c.rel_pcol = (c.qg2, c.qg2.count)


def emit_colguard(P, c, col):
    P.op("scalar", lambda e: e.copy(
        c.acol[:, col:col + 1], c.acol[:, col:col + 1]),
        reads=[c.rel_pcol, c.sc])


def emit_absmax_global(P, c, acc_col_ap, row_dst):
    P.op("gpsimd", lambda e: e.dma_start(
        c.rowtmp[0:1, 384:512], acc_col_ap, allow_slow_non_contiguous=True),
        reads=[c.ve], dma=c.qg2)
    P.op("vector", lambda e: e.tensor_reduce(
        c.rowtmp[0:1, row_dst:row_dst + 1], c.rowtmp[0:1, 384:512],
        AX.X, ALU.max), reads=[c.qg2, c.ve])


def recip127_row(P, c, src, dst):
    P.op("vector", lambda e: e.reciprocal(
        c.rowtmp[0:1, dst:dst + 1], c.rowtmp[0:1, src:src + 1]), reads=[c.ve])
    P.op("vector", lambda e: e.tensor_scalar(
        c.rowtmp[0:1, dst:dst + 1], c.rowtmp[0:1, dst:dst + 1], 127.0, None,
        ALU.mult), reads=[c.ve])


def new_ctx(nc, st0):
    sb = lambda n, s, d: st0.enter_context(nc.sbuf_tensor(n, s, d))
    c = Ctx()
    c.d_cs = nc.dram_tensor("cs_scr", [1, 16], F32).ap()
    c.ones_c = sb("ones_c", [128, 1], F32)
    c.ones_r = sb("ones_r", [1, 128], F32)
    c.cmag = sb("cmag", [128, 1], F32)
    c.zero = sb("zero", [128, 1], F32)
    c.eps = sb("eps", [128, 1], F32)
    c.acol = sb("acol", [128, 8], F32)
    c.rowtmp = sb("rowtmp", [1, 512], F32)
    c.amax = sb("amax", [128, 2], F32)
    c.rel_sq = c.rel_ps = c.rel_bc = c.rel_pcol = None
    c.rel_sq2 = [None, None]
    c.rel_pg = c.rel_pu = c.rel_pd = c.rel_sg = c.rel_sd = None
    c.rel_pd2 = [None, None]
    c.rel_sd2 = [None, None]
    c.jd2 = 0
    return c


def make_sems(nc, st0, c):
    SEM = lambda n: Sem(st0.enter_context(nc.semaphore(n)), n)
    c.esems = {e: SEM("e_" + e) for e in Prog.ENGS}
    c.sc, c.ve, c.te = c.esems["scalar"], c.esems["vector"], c.esems["tensor"]
    c.q1, c.qg, c.qg2 = SEM("q1"), SEM("qg"), SEM("qg2")
    c.qst, c.qstg = SEM("qst"), SEM("qstg")
    c.qust, c.qbc = SEM("qust"), SEM("qbc")
    c.rings = [SEM(f"rs{i}") for i in range(16)]
    c.swrings = c.rings[14:16]
    c.swrings2 = [SEM(f"rsw{i}") for i in range(16, 20)]
    c.lw = {}


def p0_alloc(nc, stp, c):
    sbp = lambda n, s, d: stp.enter_context(nc.sbuf_tensor(n, s, d))
    c.q_wraw = sbp("wraw", [128, 2, DM], F32)
    c.q_wq16 = sbp("wq16", [128, 2, DM], BF16)
    c.q_cwbuf = sbp("cwbuf", [128, NB_DI * DC], F32)
    c.qscr = sbp("qscr", [128, 1024], F32)


def quant_weight2(P, c, raw_ap, n, recip_col, codes_dst, parts=128, extra=()):
    for k0 in range(0, n, 1024):
        w = min(1024, n - k0)
        P.op("scalar", lambda e, k0=k0, w=w: e.activation(
            c.qscr[0:parts, 0:w], raw_ap[:, k0:k0 + w], AF.Identity,
            bias=c.cmag[0:parts, 0:1], scale=recip_col),
            reads=list(extra) + [c.sc, c.ve, c.q1])
        P.op("vector", lambda e, w=w: e.tensor_scalar(
            c.qscr[0:parts, 0:w], c.qscr[0:parts, 0:w], CM, 127.0,
            ALU.subtract, ALU.min), reads=[c.sc, c.ve])
        P.op("vector", lambda e, k0=k0, w=w: e.tensor_scalar(
            codes_dst[:, k0:k0 + w], c.qscr[0:parts, 0:w], -128.0, None,
            ALU.max), reads=[c.ve])


def phase_p0(nc, c, D, load_x=True):
    with ExitStack() as stp:
        cwbuf = c.q_cwbuf
        P = Prog(nc, c.esems, c.lw)
        P.op("vector", lambda e: e.memset(c.cmag[:], CM))
        P.op("vector", lambda e: e.memset(c.zero[:], 0.0))
        P.op("vector", lambda e: e.memset(c.eps[:], EPS))
        P.op("vector", lambda e: e.memset(c.amax[:, 0:1], 0.0))
        if load_x:
            for b in range(NB_DM):
                P.op("sync", lambda e, b=b: e.dma_start(
                    c.resid[:, b, :], D.x[bass.ts(b, 128), :]), dma=c.q1)
        P.op("sync", lambda e: e.dma_start(c.ones_c[:], D.oc), dma=c.q1)
        P.op("sync", lambda e: e.dma_start(c.ones_r[:], D.orr), dma=c.q1)
        P.op("sync", lambda e: e.dma_start(c.cb[:], D.cbn), dma=c.q1)
        P.op("sync", lambda e: e.dma_start(cwbuf[:], D.cw), dma=c.q1)
        for (src, dst, mark, sem) in (
                (D.wg, D.wgq, "m_wg", c.qg), (D.wu, D.wuq, "m_wu", c.qstg),
                (D.wd, D.wdq, "m_wd", c.qg2)):
            P.op("gpsimd", lambda e, s=src, d=dst: e.dma_start(d, s), dma=sem)
            setattr(c, mark, (sem, sem.count))
        with nc.Block() as blk:
            P.realize(blk)


def p0_quant_steps(nc, c, D):
    wraw, wq16 = c.q_wraw, c.q_wq16
    cwbuf = c.q_cwbuf
    rW = Ring(2, c.swrings2)
    rS = Ring(2, c.swrings)
    lds = {}
    steps = []

    def loadW(P, wc):
        j, rel = rW.acquire()
        P.op("gpsimd", lambda e, wc=wc, j=j: e.dma_start(
            wraw[:, j, :], D.win[wc]), reads=[rel], dma=rW.sems[j])
        lds[wc] = (j, (rW.sems[j], rW.sems[j].count))

    def redA(P, wc):
        j, ld = lds.pop(wc)
        P.op("vector", lambda e, j=j: e.tensor_reduce(
            c.amax[:, 1:2], wraw[:, j, :], AX.X, ALU.max,
            apply_absolute_value=True), reads=[ld, c.ve])
        r2 = P.op("vector", lambda e: e.tensor_tensor(
            c.amax[:, 0:1], c.amax[:, 0:1], c.amax[:, 1:2], ALU.max),
            reads=[c.ve])
        rW.release(j, r2)

    steps.append(lambda P: (loadW(P, 0), loadW(P, 1)))
    for wc in range(NB_XZ):
        def stepA(P, wc=wc):
            redA(P, wc)
            if wc + 2 < NB_XZ:
                loadW(P, wc + 2)
        steps.append(stepA)

    def finA(P):
        emit_absmax_global(P, c, c.amax[:, 0:1], 0)
        recip127_row(P, c, 0, 1)
        P.op("vector", lambda e: e.tensor_scalar(
            c.rowtmp[0:1, 2:3], c.rowtmp[0:1, 0:1],
            float(S_IN / S_XZ / 127.0), None, ALU.mult), reads=[c.ve])
        emit_colscale(P, c, c.rowtmp[0:1, 1:2], 0, 0)
        emit_colscale(P, c, c.rowtmp[0:1, 2:3], 1, 1)
        emit_colguard(P, c, 1)
    steps.append(finA)

    def procB(P, wc):
        j, ld = lds.pop(wc)
        js, rels = rS.acquire()
        P.op("scalar", lambda e, j=j: e.activation(
            c.qscr[:, 0:DM], wraw[:, j, :], AF.Identity,
            bias=c.cmag[:, 0:1], scale=c.acol[:, 0:1]),
            reads=[ld, c.sc, c.ve, rels])
        rW.release(j, (c.sc, P.es["scalar"].count))
        P.op("vector", lambda e: e.tensor_scalar(
            c.qscr[:, 0:DM], c.qscr[:, 0:DM], CM, 127.0,
            ALU.subtract, ALU.min), reads=[c.sc, c.ve])
        P.op("vector", lambda e, js=js: e.tensor_scalar(
            wq16[:, js, :], c.qscr[:, 0:DM], -128.0, None,
            ALU.max), reads=[c.ve])
        P.op("gpsimd", lambda e, wc=wc, js=js: e.dma_start(
            D.winq[wc], wq16[:, js, :]), reads=[c.ve], dma=rS.sems[js])
        rS.release(js, (rS.sems[js], rS.sems[js].count))
        if wc + 2 < NB_XZ:
            loadW(P, wc + 2)
    stepsB = [lambda P: (loadW(P, 0), loadW(P, 1))]
    for wc in range(NB_XZ):
        stepsB.append(lambda P, wc=wc: procB(P, wc))

    wx_pn = D.wx.rearrange("b p n -> p b n")
    wxq_f = c.wxq[:].rearrange("p b n -> p (b n)")
    wdt_pn = D.wdt.rearrange("b p n -> p b n")
    wdtq_f = c.wdtq[:].rearrange("p b n -> p (b n)")

    def ld_wx(P, half):
        P.op("gpsimd", lambda e, half=half: e.dma_start(
            c.qscr[:, 0:768].rearrange("p (b n) -> p b n", n=96),
            wx_pn[:, half * 8:half * 8 + 8, :]), reads=[c.ve, c.sc],
            dma=c.swrings2[0])
        return (c.swrings2[0], c.swrings2[0].count)

    def ld_wdt(P, half):
        P.op("gpsimd", lambda e, half=half: e.dma_start(
            c.qscr[0:64, 0:1024].rearrange("p (b n) -> p b n", n=128),
            wdt_pn[:, half * 8:half * 8 + 8, :]), reads=[c.ve, c.sc],
            dma=c.swrings2[0])
        return (c.swrings2[0], c.swrings2[0].count)

    def stepC1(P, half):
        ld = ld_wx(P, half)
        P.op("vector", lambda e, half=half: e.tensor_reduce(
            c.amax[:, half:half + 1], c.qscr[:, 0:768], AX.X,
            ALU.max, apply_absolute_value=True), reads=[ld, c.ve])
    steps.append(lambda P: stepC1(P, 0))

    def stepC1b(P):
        stepC1(P, 1)
        P.op("vector", lambda e: e.tensor_tensor(
            c.amax[:, 0:1], c.amax[:, 0:1], c.amax[:, 1:2], ALU.max),
            reads=[c.ve])
    steps.append(stepC1b)

    def stepC2(P):
        emit_absmax_global(P, c, c.amax[:, 0:1], 3)
        recip127_row(P, c, 3, 4)
        P.op("vector", lambda e: e.tensor_scalar(
            c.rowtmp[0:1, 5:6], c.rowtmp[0:1, 3:4],
            float(S_CONV / S_XDBL / 127.0), None, ALU.mult), reads=[c.ve])
        emit_colscale(P, c, c.rowtmp[0:1, 4:5], 2, 0)
        emit_colscale(P, c, c.rowtmp[0:1, 5:6], 3, 1)
        emit_colguard(P, c, 3)
    steps.append(stepC2)

    def quantC(P, half, n, ld_fn, recip_col, codes_f, parts):
        ld = ld_fn(P, half)
        P.op("scalar", lambda e, n=n, parts=parts: e.activation(
            c.qscr[0:parts, 0:n], c.qscr[0:parts, 0:n], AF.Identity,
            bias=c.cmag[0:parts, 0:1], scale=recip_col),
            reads=[ld, c.sc, c.ve])
        P.op("vector", lambda e, n=n, parts=parts: e.tensor_scalar(
            c.qscr[0:parts, 0:n], c.qscr[0:parts, 0:n], CM, 127.0,
            ALU.subtract, ALU.min), reads=[c.sc, c.ve])
        P.op("vector", lambda e, half=half, n=n, parts=parts: e.tensor_scalar(
            codes_f[:, half * n:half * n + n], c.qscr[0:parts, 0:n],
            -128.0, None, ALU.max), reads=[c.ve])
    for half in (0, 1):
        steps.append(lambda P, half=half: quantC(
            P, half, 768, ld_wx, c.acol[:, 2:3], wxq_f, 128))

    def stepC4(P, half):
        ld = ld_wdt(P, half)
        P.op("vector", lambda e, half=half: e.tensor_reduce(
            c.amax[0:64, half:half + 1], c.qscr[0:64, 0:1024], AX.X,
            ALU.max, apply_absolute_value=True), reads=[ld, c.ve])
    steps.append(lambda P: stepC4(P, 0))

    def stepC4b(P):
        stepC4(P, 1)
        P.op("vector", lambda e: e.tensor_tensor(
            c.amax[0:64, 0:1], c.amax[0:64, 0:1], c.amax[0:64, 1:2],
            ALU.max), reads=[c.ve])
        P.op("vector", lambda e: e.memset(c.amax[64:128, 0:1], 0.0),
             reads=[c.ve])
    steps.append(stepC4b)

    def stepC5(P):
        emit_absmax_global(P, c, c.amax[:, 0:1], 6)
        recip127_row(P, c, 6, 7)
        P.op("vector", lambda e: e.tensor_scalar(
            c.rowtmp[0:1, 8:9], c.rowtmp[0:1, 6:7], float(1.0 / 127.0),
            None, ALU.mult), reads=[c.ve])
        emit_colscale(P, c, c.rowtmp[0:1, 7:8], 4, 0)
        emit_colscale(P, c, c.rowtmp[0:1, 8:9], 5, 1)
        emit_colguard(P, c, 5)
    steps.append(stepC5)
    for half in (0, 1):
        steps.append(lambda P, half=half: quantC(
            P, half, 1024, ld_wdt, c.acol[0:64, 4:5], wdtq_f, 64))

    def stepD1(P):
        P.op("vector", lambda e: e.tensor_reduce(
            c.amax[:, 0:1], cwbuf[:], AX.X, ALU.max,
            apply_absolute_value=True), reads=[c.q1, c.ve])
        emit_absmax_global(P, c, c.amax[:, 0:1], 9)
        recip127_row(P, c, 9, 10)
        P.op("vector", lambda e: e.tensor_scalar(
            c.rowtmp[0:1, 11:12], c.rowtmp[0:1, 9:10], float(S_XZ / 127.0),
            None, ALU.mult), reads=[c.ve])
        emit_colscale(P, c, c.rowtmp[0:1, 10:11], 6, 0)
        emit_colscale(P, c, c.rowtmp[0:1, 11:12], 7, 1)
        emit_colguard(P, c, 7)
    steps.append(stepD1)

    def stepD2(P):
        quant_weight2(P, c, cwbuf[:], NB_DI * DC, c.acol[:, 6:7],
                      c.qscr[:, 0:NB_DI * DC])
        P.op("vector", lambda e: e.tensor_scalar(
            c.taps[:], c.qscr[:, 0:NB_DI * DC], c.acol[:, 7:8],
            None, ALU.mult), reads=[c.ve, c.sc])
    steps.append(stepD2)
    steps.extend(stepsB)
    return steps


def phase_ffn1(nc, c, D, NT, subs_all, steps=()):
    with ExitStack() as stp:
        sbp = lambda n, s, d: stp.enter_context(nc.sbuf_tensor(n, s, d))
        psp = lambda n, s: stp.enter_context(nc.psum_tensor(n, s, F32))
        c.sq8 = sbp("sq8", [128, 2, 512], F32)
        c.xln = sbp("xln", [128, NB_DM, 1028], BF16)
        c.h_ffn = sbp("h_ffn", [128, NB_HF, 1028], BF16)
        c.wbufA = sbp("wbufA", [128, 2, DM], BF16)
        c.wbufB = sbp("wbufB", [128, 2, DM], BF16)
        c.wdbuf = sbp("wdbuf", [128, 2, HF], BF16)
        c.p_sum = psp("p_sum", [1, 512])
        c.p_sq = psp("p_sq", [1, 512])
        c.p_g = psp("p_g", [128, 2, 512])
        c.p_u = psp("p_u", [128, 2, 512])
        c.p_d = psp("p_d", [128, 2, 512])
        c.p_bc1 = c.p_g[:, 0, :]
        c.p_bc2 = c.p_g[:, 1, :]
        c.rA = Ring(2, c.rings[4:6])
        c.rB = Ring(2, c.rings[6:8])
        c.rD = Ring(2, c.rings[8:10])
        P = Prog(nc, c.esems, c.lw)
        c.resid_ready = (c.q1, c.q1.count)
        emit_ffn(P, c, split_supers(subs_all, 1028), D.wgq, D.wuq, D.wdq,
                 steps=steps)
        for b in range(NB_DM):
            P.op("sync", lambda e, b=b: e.dma_start(
                D.x1_dst(b), c.resid[:, b, HALO:NT]),
                reads=[c.resid_ready], dma=c.qst)
        P.drain(c.rings + [c.qg2])
        with nc.Block() as blk:
            P.realize(blk)


def mamba_in_alloc(nc, c, stp, M):
    sbp = lambda n, s, d: stp.enter_context(nc.sbuf_tensor(n, s, d))
    psp = lambda n, s: stp.enter_context(nc.psum_tensor(n, s, F32))
    c.sq8 = sbp("sq8b", [128, 2, 512], F32)
    M.u0s = sbp("u0s", [128, 2, NB_DM, 512], BF16)
    M.xroll = sbp("xroll", [128, NB_DI, 516], BF16)
    M.usub = sbp("usub", [128, NB_DI, 512], BF16)
    M.xdblc = sbp("xdblc", [96, 512], BF16)
    M.winb = sbp("winb", [128, 3, DM], BF16)
    M.zstg = sbp("zstg", [128, 2, 512], BF16)
    M.dstg = sbp("dstg", [128, 2, 512], BF16)
    M.convacc = sbp("convacc", [128, 2, 512], F32)
    c.p_sum = psp("p_sum2", [1, 512])
    c.p_sq = psp("p_sq2", [1, 512])
    M.p_w = psp("p_w", [128, 3, 512])
    c.p_bc1 = M.p_w[:, 0, :]
    c.p_bc2 = M.p_w[:, 1, :]
    M.p_x = psp("p_x", [96, 512])
    M.p_dt = psp("p_dt", [128, 2, 512])
    M.rWi = Ring(3, c.rings[0:3])
    M.rZ = Ring(2, c.swrings2[0:2])
    M.rDt = Ring(2, c.swrings2[2:4])
    M.rel_pw = [None, None, None]
    M.rel_pdt = [None, None]
    M.rel_cacc = [None, None]
    M.rel_ust = None
    M.rel_px = M.rel_usub = None
    M.rel_xdbl = []
    M.prev_fx = None
    M.first = True


def emit_mamba_subs(nc, c, D, P, M, sub_list, flush=True):
    u0s, xroll, usub, xdblc = M.u0s, M.xroll, M.usub, M.xdblc
    winb, zstg, dstg, convacc = M.winb, M.zstg, M.dstg, M.convacc
    p_w, p_x, p_dt = M.p_w, M.p_x, M.p_dt
    rWi, rZ, rDt = M.rWi, M.rZ, M.rDt
    if M.first:
        c.rel_sq = c.rel_ps = c.rel_bc = None
        c.rel_sq2 = [None, None]

    def conv_p1(b, w):
        js = b % 2
        P.op("vector", lambda e, b=b, js=js, w=w: e.tensor_scalar(
            convacc[:, js, 0:w], xroll[:, b, 1:1 + w],
            c.taps[:, b * DC:b * DC + 1], c.cb[:, b:b + 1],
            ALU.mult, ALU.add), reads=[M.rel_usub, M.rel_cacc[js], c.ve])
        for k in (1, 2, 3):
            P.op("vector", lambda e, b=b, k=k, js=js, w=w: e.scalar_tensor_tensor(
                convacc[:, js, 0:w], xroll[:, b, 1 + k:1 + k + w],
                c.taps[:, b * DC + k:b * DC + k + 1],
                convacc[:, js, 0:w], ALU.mult, ALU.add), reads=[c.ve])
        P.op("scalar", lambda e, js=js, w=w: e.activation(
            c.stage[:, 1024 + js * 512:1024 + js * 512 + w],
            convacc[:, js, 0:w], AF.Silu,
            bias=c.zero[:, 0:1]), reads=[c.ve, c.sc])
        M.rel_cacc[js] = (c.sc, P.es["scalar"].count)

    def conv_p2(b, w, own_off):
        js = b % 2
        P.op("vector", lambda e, js=js, w=w: e.tensor_scalar(
            c.stage[:, 2048 + js * 512:2048 + js * 512 + w],
            c.stage[:, 1024 + js * 512:1024 + js * 512 + w],
            1.0 / S_CONV, CM, ALU.mult, ALU.add), reads=[c.sc, c.ve])
        P.op("vector", lambda e, js=js, w=w: e.tensor_scalar(
            c.stage[:, 2048 + js * 512:2048 + js * 512 + w],
            c.stage[:, 2048 + js * 512:2048 + js * 512 + w],
            CM, 127.0, ALU.subtract, ALU.min), reads=[c.ve])
        P.op("vector", lambda e, b=b, js=js, w=w: e.tensor_scalar(
            usub[:, b, 0:w], c.stage[:, 2048 + js * 512:2048 + js * 512 + w],
            -128.0, None, ALU.max), reads=[c.ve, M.rel_ust])
        P.op("gpsimd", lambda e, b=b, w=w, oo=own_off: e.dma_start(
            D.u_dst(b, oo, w), usub[:, b, 0:w]),
            reads=[c.ve], dma=c.qust)

    for (si, (off, w)) in sub_list:
        own_off = off - HALO
        is_halo = (si == 0)
        j2 = si % 2
        pend_cv = None
        if not is_halo:
            pw_ = M.prev_fx
            P.op("vector", lambda e, pw_=pw_: e.tensor_copy(
                xroll[:, :, 0:4], xroll[:, :, pw_:pw_ + 4]), reads=[c.ve])

        def apply2(b, off, w):
            P.op("vector", lambda e, b=b, off=off, w=w: e.tensor_tensor(
                c.stage[:, 1024:1024 + w], c.resid[:, b, off:off + w],
                c.p_bc1[:, 0:w], ALU.mult), reads=[c.te, c.ve])
            P.op("vector", lambda e, w=w: e.tensor_tensor(
                c.stage[:, 1536:1536 + w], c.stage[:, 1024:1024 + w],
                c.p_bc2[:, 0:w], ALU.subtract), reads=[c.ve])
            P.op("vector", lambda e, w=w: e.tensor_scalar(
                c.stage[:, 2048:2048 + w], c.stage[:, 1536:1536 + w],
                CM, CM, ALU.add, ALU.subtract), reads=[c.ve])
            P.op("vector", lambda e, b=b, j2=j2, w=w: e.tensor_scalar(
                u0s[:, j2, b, 0:w], c.stage[:, 2048:2048 + w], 127.0,
                -128.0, ALU.min, ALU.max), reads=[c.ve])
        emit_ln(P, c, [(off, w)], apply2, fq_scale=1.0 / S_IN)
        u0_done = (c.ve, P.es["vector"].count)

        nwc = NB_DI if is_halo else NB_XZ
        for wc in range(nwc):
            jw = wc % 3
            j, rel = rWi.acquire()
            P.op("sync", lambda e, wc=wc, j=j: e.dma_start(
                winb[:, j, :], D.winq[wc]), reads=[rel], dma=rWi.sems[j])
            ld = (rWi.sems[j], rWi.sems[j].count)
            for dc in range(NB_DM):
                P.op("tensor", lambda e, dc=dc, j=j, j2=j2, w=w, jw=jw: e.matmul(
                    p_w[:, jw, 0:w], winb[:, j, dc * 128:(dc + 1) * 128],
                    u0s[:, j2, dc, 0:w], start=(dc == 0),
                    stop=(dc == NB_DM - 1)), reads=[ld, u0_done, M.rel_pw[jw]])
            rWi.release(j, (c.te, P.es["tensor"].count))
            if wc < NB_DI:
                M.rel_pw[jw] = emit_fq3(
                    P, c, xroll[:, wc, 4:4 + w],
                    p_w[:, jw, 0:w], w, c.acol[:, 1:2])
                if not is_halo:
                    conv_p1(wc, w)
                    if pend_cv is not None:
                        conv_p2(pend_cv, w, own_off)
                    pend_cv = wc
            else:
                if pend_cv is not None:
                    conv_p2(pend_cv, w, own_off)
                    pend_cv = None
                jz, relz = rZ.acquire()
                M.rel_pw[jw] = emit_fq3(
                    P, c, zstg[:, jz, 0:w], p_w[:, jw, 0:w],
                    w, c.acol[:, 1:2], extra_reads=[relz])
                P.op("gpsimd", lambda e, wc=wc, jz=jz, w=w, oo=own_off: e.dma_start(
                    D.z_dst(wc - NB_DI, oo, w), zstg[:, jz, 0:w]),
                    reads=[c.ve], dma=rZ.sems[jz])
                rZ.release(jz, (rZ.sems[jz], rZ.sems[jz].count))
        M.prev_fx = w
        if is_halo:
            continue
        if pend_cv is not None:
            conv_p2(pend_cv, w, own_off)
            pend_cv = None
        M.rel_ust = (c.qust, c.qust.count)
        u_done = (c.ve, P.es["vector"].count)

        for dc in range(NB_DI):
            P.op("tensor", lambda e, dc=dc, w=w: e.matmul(
                p_x[:, 0:w], c.wxq[:, dc, :], usub[:, dc, 0:w],
                start=(dc == 0), stop=(dc == NB_DI - 1)),
                reads=[u_done, M.rel_px])
        M.rel_usub = (c.te, P.es["tensor"].count)
        M.rel_px = emit_fq3(P, c, xdblc[:, 0:w], p_x[:, 0:w], w,
                            c.acol[0:96, 3:4], parts=96,
                            extra_reads=M.rel_xdbl)
        xdbl_done = (c.ve, P.es["vector"].count)
        P.op("gpsimd", lambda e, w=w, oo=own_off: e.dma_start(
            D.B_dst(oo, w), xdblc[64:80, 0:w]),
            reads=[xdbl_done], dma=c.qbc)
        P.op("gpsimd", lambda e, w=w, oo=own_off: e.dma_start(
            D.C_dst(oo, w), xdblc[80:96, 0:w]),
            reads=[xdbl_done], dma=c.qbc)
        bc_rel = (c.qbc, c.qbc.count)
        banks = [(p_dt[:, 0, :], ("pdt", 0)), (p_dt[:, 1, :], ("pdt", 1)),
                 (p_w[:, 0, :], ("pw", 0)), (p_w[:, 1, :], ("pw", 1)),
                 (p_w[:, 2, :], ("pw", 2))]
        for dc in range(NB_DI):
            bank, (kind, kk) = banks[dc % 5]
            rel = M.rel_pdt[kk] if kind == "pdt" else M.rel_pw[kk]
            P.op("tensor", lambda e, dc=dc, bank=bank, w=w: e.matmul(
                bank[:, 0:w], c.wdtq[:, dc, :], xdblc[0:64, 0:w],
                start=True, stop=True), reads=[xdbl_done, rel])
            jd, reld = rDt.acquire()
            newrel = emit_fq3(P, c, dstg[:, jd, 0:w],
                              bank[:, 0:w], w, c.acol[:, 5:6],
                              extra_reads=[reld])
            if kind == "pdt":
                M.rel_pdt[kk] = newrel
            else:
                M.rel_pw[kk] = newrel
            P.op("gpsimd", lambda e, dc=dc, jd=jd, w=w, oo=own_off: e.dma_start(
                D.dtq_dst(dc, oo, w), dstg[:, jd, 0:w]),
                reads=[c.ve], dma=rDt.sems[jd])
            rDt.release(jd, (rDt.sems[jd], rDt.sems[jd].count))
        M.rel_xdbl = [(c.te, P.es["tensor"].count), bc_rel]
    M.first = False


def phase_mamba_in(nc, c, D, NT, subs_all):
    with ExitStack() as stp:
        M = Ctx()
        mamba_in_alloc(nc, c, stp, M)
        P = Prog(nc, c.esems, c.lw)
        emit_mamba_subs(nc, c, D, P, M, list(enumerate(subs_all)))
        P.drain([c.qst, c.qstg, c.qust, c.qbc] + c.rings + c.swrings2)
        with nc.Block() as blk:
            P.realize(blk)


def phase_scan(nc, c, D, nb_lane, seq_len):
    TC = SC_TC
    NCH = seq_len // TC
    with ExitStack() as stp:
        sb = lambda n, s, d: stp.enter_context(nc.sbuf_tensor(n, s, d))
        abuf = sb("abuf", [128, 2, DS, TC], F32)
        hbuf = sb("hbuf", [128, DS, TC], BF16)
        bbuf = sb("bbuf", [128, DS, TC], BF16)
        yv = sb("yv", [128, DS, TC], BF16)
        brep = sb("brep", [128, 2, DS, TC], BF16)
        crep = sb("crep", [128, 2, DS, TC], BF16)
        ubuf = sb("ubuf", [128, 2, TC], BF16)
        dtbuf = sb("dtbuf", [128, 2, TC], BF16)
        dt32 = sb("dt32", [128, 2, TC], F32)
        spa = sb("spa", [128, 2, TC], F32)
        spb = sb("spb", [128, 2, TC], F32)
        spc = sb("spc", [128, TC], F32)
        wbf = sb("wbf", [128, TC], BF16)
        ystg = sb("ystg", [128, 4, TC], F32)
        hcar = sb("hcar", [128, nb_lane, DS], F32)
        alog = sb("alog", [128, nb_lane * DS], F32)
        Acol = sb("Acol", [128, nb_lane * DS], F32)
        dtb = sb("dtb", [128, nb_lane], F32)
        dD4 = sb("dD4", [128, nb_lane], F32)
        onec = sb("onec", [128, 1], F32)
        zeroc = sb("zeroc", [128, 1], F32)
        sc, ve, gp = c.sc, c.ve, c.esems["gpsimd"]
        q1 = c.q1
        rsB = c.rings[0:2]
        rsC = c.rings[2:4]
        rsU = c.rings[4:6]
        rsD = c.rings[6:8]
        rsY = c.swrings + c.swrings2[0:2]
        relB = [None, None]
        relC = [None, None]
        relU = [None, None]
        relD = [None, None]
        relSP = [None, None]
        pend_store = [None]
        relDT = [None, None]
        relA = [None, None]
        P = Prog(nc, c.esems, c.lw)

        P.op("vector", lambda e: e.memset(hcar[:], 0.0))
        P.op("vector", lambda e: e.memset(onec[:], 1.0))
        P.op("vector", lambda e: e.memset(zeroc[:], 0.0))
        P.op("sync", lambda e: e.dma_start(alog[:], D.al), dma=q1)
        P.op("sync", lambda e: e.dma_start(dtb[:], D.dtb), dma=q1)
        P.op("sync", lambda e: e.dma_start(dD4[:], D.dD), dma=q1)
        P.op("scalar", lambda e: e.activation(
            Acol[:], alog[:], AF.Exp, bias=zeroc[:, 0:1]), reads=[q1, ve, sc])
        P.op("vector", lambda e: e.tensor_scalar(
            Acol[:], Acol[:], -1.0, None, ALU.mult), reads=[sc, ve])
        P.op("vector", lambda e: e.tensor_scalar(
            dD4[:], dD4[:], float(S_CONV), None, ALU.mult), reads=[q1, ve])
        acol_done = (c.ve, P.es["vector"].count)

        for ch in range(NCH):
            t0 = ch * TC
            jB = ch % 2
            P.op("sync", lambda e, t0=t0, jB=jB: e.dma_start(
                brep[:, jB], D.b_src(t0).unsqueeze(0).broadcast_to(
                    [128, DS, TC])), reads=[relB[jB]], dma=rsB[jB])
            ldB = (rsB[jB], rsB[jB].count)
            P.op("sync", lambda e, t0=t0, jB=jB: e.dma_start(
                crep[:, jB], D.c_src(t0).unsqueeze(0).broadcast_to(
                    [128, DS, TC])), reads=[relC[jB]], dma=rsC[jB])
            ldC = (rsC[jB], rsC[jB].count)
            for db in range(nb_lane):
                it = ch * nb_lane + db
                js = it % 2
                jU = js
                jY = it % 4
                P.op("scalar", lambda e, t0=t0, db=db, jU=jU: e.dma_start(
                    ubuf[:, jU], D.u_src(db, t0)), reads=[relU[jU]],
                    dma=rsU[jU])
                ldU = (rsU[jU], rsU[jU].count)
                P.op("scalar", lambda e, t0=t0, db=db, jU=jU: e.dma_start(
                    dtbuf[:, jU], D.dt_src(db, t0)), reads=[relD[jU]],
                    dma=rsD[jU])
                ldD = (rsD[jU], rsD[jU].count)
                P.op("scalar", lambda e, db=db, jU=jU, js=js: e.activation(
                    spa[:, js], dtbuf[:, jU], AF.Relu, bias=dtb[:, db:db + 1],
                    scale=float(S_DT)), reads=[ldD, q1, sc, relSP[js]])
                P.op("scalar", lambda e, db=db, jU=jU, js=js: e.activation(
                    spb[:, js], dtbuf[:, jU], AF.Abs, bias=dtb[:, db:db + 1],
                    scale=float(S_DT)), reads=[sc])
                relD[jU] = (sc, P.es["scalar"].count)
                P.op("scalar", lambda e, js=js: e.activation(
                    spc[:], spb[:, js], AF.Exp, scale=-1.0), reads=[sc])
                P.op("scalar", lambda e, js=js: e.activation(
                    spb[:, js], spc[:], AF.Ln, bias=onec[:, 0:1]), reads=[sc])
                sp_done = (sc, P.es["scalar"].count)
                P.op("gpsimd", lambda e, js=js: e.tensor_tensor(
                    dt32[:, js], spa[:, js], spb[:, js], ALU.add),
                    reads=[sp_done, gp, relDT[js]])
                dt_done = (gp, P.es["gpsimd"].count)
                relSP[js] = dt_done
                if pend_store[0] is not None:
                    pdb, pt0, pjY, pvc = pend_store[0]
                    P.op("gpsimd", lambda e, pdb=pdb, pt0=pt0, pjY=pjY: e.dma_start(
                        D.y_dst(pdb, pt0), ystg[:, pjY]),
                        reads=[(c.ve, pvc)], dma=rsY[pjY])
                    pend_store[0] = None
                for n in range(DS):
                    P.op("scalar", lambda e, db=db, n=n, js=js: e.activation(
                        abuf[:, js, n, :], dt32[:, js], AF.Exp,
                        scale=Acol[:, db * DS + n:db * DS + n + 1]),
                        reads=[dt_done, relA[js], sc, acol_done])
                a_done = (sc, P.es["scalar"].count)
                P.op("vector", lambda e, jU=jU, js=js: e.scalar_tensor_tensor(
                    wbf[:], ubuf[:, jU], float(S_CONV * S_XDBL), dt32[:, js],
                    ALU.mult, ALU.mult), reads=[ldU, dt_done, ve, acol_done])
                wdone = (c.ve, P.es["vector"].count)
                P.op("vector", lambda e, jB=jB: e.tensor_tensor(
                    bbuf[:], wbf[:].unsqueeze(1).broadcast_to([128, DS, TC]),
                    brep[:, jB], ALU.mult), reads=[ldB, ve])
                relB[jB] = (ve, P.es["vector"].count)
                for n in range(DS):
                    P.op("vector", lambda e, db=db, n=n, js=js: e.tensor_tensor_scan(
                        hbuf[:, n, :], abuf[:, js, n, :], bbuf[:, n, :],
                        hcar[:, db, n:n + 1], ALU.mult, ALU.add),
                        reads=[a_done, ve])
                scans_done = (c.ve, P.es["vector"].count)
                relA[js] = scans_done
                relDT[js] = (wdone, scans_done) if False else scans_done
                P.op("vector", lambda e, db=db: e.tensor_copy(
                    hcar[:, db, :], hbuf[:, :, TC - 1:TC]), reads=[ve])
                P.op("vector", lambda e, jB=jB: e.tensor_tensor(
                    yv[:], hbuf[:], crep[:, jB], ALU.mult), reads=[ldC, ve])
                relC[jB] = (ve, P.es["vector"].count)
                for hw in (8, 4, 2, 1):
                    P.op("vector", lambda e, hw=hw: e.tensor_tensor(
                        yv[:, 0:hw, :], yv[:, 0:hw, :],
                        yv[:, hw:2 * hw, :], ALU.add), reads=[ve])
                P.op("vector", lambda e, jY=jY: e.tensor_scalar(
                    ystg[:, jY], yv[:, 0, :], float(S_XDBL), None,
                    ALU.mult), reads=[ve, (rsY[jY], rsY[jY].count)])
                P.op("vector", lambda e, db=db, jU=jU, jY=jY: e.scalar_tensor_tensor(
                    ystg[:, jY], ubuf[:, jU], dD4[:, db:db + 1], ystg[:, jY],
                    ALU.mult, ALU.add), reads=[ve])
                relU[jU] = (ve, P.es["vector"].count)
                pend_store[0] = (db, t0, jY, P.es["vector"].count)
        pdb, pt0, pjY, pvc = pend_store[0]
        P.op("gpsimd", lambda e, pdb=pdb, pt0=pt0, pjY=pjY: e.dma_start(
            D.y_dst(pdb, pt0), ystg[:, pjY]), reads=[(c.ve, pvc)], dma=rsY[pjY])
        P.drain(rsY + [rsB[0], rsB[1], rsC[0], rsC[1],
                 rsU[0], rsU[1], rsD[0], rsD[1]])
        with nc.Block() as blk:
            P.realize(blk)


def phase_wout_prep(nc, c, D, whq, h128b, h128f, load_consts):
    with ExitStack() as stp:
        sbp = lambda n, s, d: stp.enter_context(nc.sbuf_tensor(n, s, d))
        psp = lambda n, s: stp.enter_context(nc.psum_tensor(n, s, F32))
        s1 = sbp("s1", [128, NB_DI, DM], F32)
        wob = sbp("wob", [128, 2, DM], F32)
        tmpb = sbp("tmpb", [128, DM], F32)
        c.p_col = psp("p_col3", [128, 2])
        p_wo = psp("p_wo", [128, DM])
        P = Prog(nc, c.esems, c.lw)
        if load_consts:
            P.op("vector", lambda e: e.memset(c.cmag[:], CM))
            P.op("vector", lambda e: e.memset(c.zero[:], 0.0))
            P.op("vector", lambda e: e.memset(c.eps[:], EPS))
            P.op("sync", lambda e: e.dma_start(c.ones_c[:], D.oc), dma=c.q1)
            P.op("sync", lambda e: e.dma_start(c.ones_r[:], D.orr), dma=c.q1)
            for (src, dst) in ((D.wg, D.wgq), (D.wu, D.wuq), (D.wd, D.wdq)):
                P.op("gpsimd", lambda e, s=src, d=dst: e.dma_start(d, s),
                     dma=c.qg)
        P.op("vector", lambda e: e.memset(c.amax[:, 0:1], 0.0), reads=[c.ve])
        for b in range(NB_DM):
            P.op("sync", lambda e, b=b: e.dma_start(
                c.resid[:, b, :], D.x1_src(b)), dma=c.q1)
        P.op("sync", lambda e: e.dma_start(h128b[:], D.h128b), dma=c.q1)
        P.op("sync", lambda e: e.dma_start(h128f[:], D.h128f), dma=c.q1)
        rW = Ring(2, c.rings[0:2])
        rel_pwo = None
        for b in range(NB_DI):
            j, rel = rW.acquire()
            P.op("sync", lambda e, b=b, j=j: e.dma_start(
                wob[:, j, :], D.wo[b]), reads=[rel], dma=rW.sems[j])
            ld = (rW.sems[j], rW.sems[j].count)
            for hh in range(2):
                P.op("tensor", lambda e, j=j, hh=hh: e.matmul(
                    p_wo[:, hh * 512:(hh + 1) * 512], h128f[:],
                    wob[:, j, hh * 512:(hh + 1) * 512],
                    start=True, stop=True), reads=[ld, c.q1, rel_pwo])
            rW.release(j, (c.te, P.es["tensor"].count))
            P.op("scalar", lambda e, b=b: e.copy(s1[:, b, :], p_wo[:]),
                 reads=[c.te, c.sc])
            rel_pwo = (c.sc, P.es["scalar"].count)
        s1_done = rel_pwo
        for lvl in (8, 4, 2, 1):
            for g0 in range(0, NB_DI, 2 * lvl):
                for i in range(lvl):
                    A_, B_ = g0 + i, g0 + i + lvl
                    P.op("vector", lambda e, A_=A_, B_=B_: e.tensor_tensor(
                        tmpb[:], s1[:, A_, :], s1[:, B_, :], ALU.subtract),
                        reads=[s1_done, c.ve])
                    P.op("vector", lambda e, A_=A_, B_=B_: e.tensor_tensor(
                        s1[:, A_, :], s1[:, A_, :], s1[:, B_, :], ALU.add),
                        reads=[c.ve])
                    P.op("vector", lambda e, B_=B_: e.tensor_copy(
                        s1[:, B_, :], tmpb[:]), reads=[c.ve])
        for b in range(NB_DI):
            P.op("vector", lambda e, b=b: e.tensor_reduce(
                c.amax[:, 1:2], s1[:, b, :], AX.X, ALU.max,
                apply_absolute_value=True), reads=[c.ve])
            P.op("vector", lambda e: e.tensor_tensor(
                c.amax[:, 0:1], c.amax[:, 0:1], c.amax[:, 1:2], ALU.max),
                reads=[c.ve])
        emit_absmax_global(P, c, c.amax[:, 0:1], 0)
        recip127_row(P, c, 0, 1)
        P.op("vector", lambda e: e.tensor_scalar(
            c.rowtmp[0:1, 2:3], c.rowtmp[0:1, 0:1],
            float(S_Y / 127.0 / np.sqrt(DI)), None, ALU.mult), reads=[c.ve])
        emit_colscale(P, c, c.rowtmp[0:1, 1:2], 0, 0)
        emit_colscale(P, c, c.rowtmp[0:1, 2:3], 1, 1)
        emit_colguard(P, c, 1)
        for b in range(NB_DI):
            quant_weight(P, c, s1[:, b, :], DM, c.acol[:, 0:1], whq[:, b, :])
        P.drain(c.rings + [c.q1, c.qg2, c.qg])
        with nc.Block() as blk:
            P.realize(blk)


def phase_mix(nc, c, D, NT, whq, h128b):
    with ExitStack() as stp:
        sbp = lambda n, s, d: stp.enter_context(nc.sbuf_tensor(n, s, d))
        psp = lambda n, s: stp.enter_context(nc.psum_tensor(n, s, F32))
        ybuf = sbp("ybuf", [128, NB_DI, 256], F32)
        zbuf = sbp("zbuf", [128, NB_DI, 256], BF16)
        yq = sbp("yq", [128, NB_DI, 256], BF16)
        f1 = sbp("f1", [128, NB_DI, 256], F32)
        f2 = sbp("f2", [128, NB_DI, 256], F32)
        yh = sbp("yh", [128, NB_DI, 256], BF16)
        p_all = psp("p_all", [128, NB_DI, 256])
        rY = Ring(1, c.rings[2:3])
        rZ = Ring(1, c.rings[4:5])
        P = Prog(nc, c.esems, c.lw)
        st = Ctx()
        st.rel_yq = st.rel_f = st.rel_po = None
        tiles = [(off, min(256, NT - off)) for off in range(0, NT, 256)]

        def pre(t):
            off, w = tiles[t]
            jy, rely = rY.acquire()
            P.op("sync", lambda e, jy=jy, off=off, w=w: e.dma_start(
                ybuf[:, :, 0:w], D.y_src(off, w)), reads=[rely],
                dma=rY.sems[jy])
            st.ldY = (rY.sems[jy], rY.sems[jy].count)
            jz, relz = rZ.acquire()
            P.op("scalar", lambda e, jz=jz, off=off, w=w: e.dma_start(
                zbuf[:, :, 0:w], D.z_src(off, w)), reads=[relz],
                dma=rZ.sems[jz])
            st.ldZ = (rZ.sems[jz], rZ.sems[jz].count)
            st.jy, st.jz = jy, jz

        def fqblock(t):
            off, w = tiles[t]
            ldY, ldZ = st.ldY, st.ldZ
            for b in range(NB_DI):
                P.op("scalar", lambda e, b=b, w=w: e.activation(
                    c.stage[:, 0:w], zbuf[:, b, 0:w], AF.Silu,
                    scale=float(S_XZ), bias=c.zero[:, 0:1]),
                    reads=[ldZ, c.sc, c.ve])
                P.op("vector", lambda e, b=b, w=w: e.tensor_tensor(
                    c.stage[:, 512:512 + w], ybuf[:, b, 0:w],
                    c.stage[:, 0:w], ALU.mult),
                    reads=[ldY, c.sc, c.ve, st.rel_yq])
                P.op("vector", lambda e, w=w: e.tensor_scalar(
                    c.stage[:, 1024:1024 + w], c.stage[:, 512:512 + w],
                    float(1.0 / S_Y), CM, ALU.mult, ALU.add), reads=[c.ve])
                P.op("vector", lambda e, w=w: e.tensor_scalar(
                    c.stage[:, 1536:1536 + w], c.stage[:, 1024:1024 + w],
                    CM, 127.0, ALU.subtract, ALU.min), reads=[c.ve])
                P.op("vector", lambda e, b=b, w=w: e.tensor_scalar(
                    yq[:, b, 0:w], c.stage[:, 1536:1536 + w], -128.0,
                    None, ALU.max), reads=[c.ve])
            done = (c.ve, P.es["vector"].count)
            rY.release(st.jy, done)
            rZ.release(st.jz, done)
            st.yq_done = done

        def s1(t):
            off, w = tiles[t]
            for b in range(NB_DI):
                P.op("tensor", lambda e, b=b, w=w: e.matmul(
                    p_all[:, b, 0:w], h128b[:], yq[:, b, 0:w],
                    start=True, stop=True), reads=[st.yq_done, c.q1, st.rel_f])
            st.rel_yq = (c.te, P.es["tensor"].count)
            for b in range(NB_DI):
                P.op("scalar", lambda e, b=b, w=w: e.copy(
                    f1[:, b, 0:w], p_all[:, b, 0:w]),
                    reads=[c.te, c.sc, c.ve])
            st.copies = (c.sc, P.es["scalar"].count)

        def btf_yh(t):
            off, w = tiles[t]
            P.waits("vector", [st.copies])
            cur, nxt = f1, f2
            for lvl in (8, 4, 2, 1):
                for g0 in range(0, NB_DI, 2 * lvl):
                    for i in range(lvl):
                        A_, B_ = g0 + i, g0 + i + lvl
                        P.op("vector", lambda e, A_=A_, B_=B_, cur=cur, nxt=nxt, w=w: e.tensor_tensor(
                            nxt[:, A_, 0:w], cur[:, A_, 0:w],
                            cur[:, B_, 0:w], ALU.add), reads=[c.ve])
                        P.op("vector", lambda e, A_=A_, B_=B_, cur=cur, nxt=nxt, w=w: e.tensor_tensor(
                            nxt[:, B_, 0:w], cur[:, A_, 0:w],
                            cur[:, B_, 0:w], ALU.subtract), reads=[c.ve])
                cur, nxt = nxt, cur
            fw_done = (c.ve, P.es["vector"].count)
            for b in range(NB_DI):
                P.op("scalar", lambda e, b=b, cur=cur, w=w: e.activation(
                    c.stage[:, 2048:2048 + w], cur[:, b, 0:w], AF.Identity,
                    scale=float(1.0 / np.sqrt(DI)), bias=c.cmag[:, 0:1]),
                    reads=[fw_done, c.sc, c.ve, c.te])
                P.op("vector", lambda e, w=w: e.tensor_scalar(
                    c.stage[:, 2560:2560 + w], c.stage[:, 2048:2048 + w], CM,
                    127.0, ALU.subtract, ALU.min), reads=[c.sc, c.ve])
                P.op("vector", lambda e, b=b, w=w: e.tensor_scalar(
                    yh[:, b, 0:w], c.stage[:, 2560:2560 + w], -128.0,
                    None, ALU.max), reads=[c.ve])
            st.yh_done = (c.ve, P.es["vector"].count)

        def wout(t):
            off, w = tiles[t]
            for oc in range(NB_DM):
                for b in range(NB_DI):
                    P.op("tensor", lambda e, oc=oc, b=b, w=w: e.matmul(
                        p_all[:, 0, 0:w], whq[:, b, oc * 128:(oc + 1) * 128],
                        yh[:, b, 0:w], start=(b == 0),
                        stop=(b == NB_DI - 1)), reads=[st.yh_done, st.rel_po])
                P.op("scalar", lambda e, w=w: e.activation(
                    c.stage[:, 3072:3072 + w], p_all[:, 0, 0:w],
                    AF.Identity, scale=c.acol[:, 1:2], bias=c.zero[:, 0:1]),
                    reads=[c.te, c.sc])
                P.op("vector", lambda e, oc=oc, off=off, w=w: e.tensor_tensor(
                    c.resid[:, oc, off:off + w],
                    c.resid[:, oc, off:off + w],
                    c.stage[:, 3072:3072 + w], ALU.add), reads=[c.sc, c.ve])
                st.rel_po = (c.ve, P.es["vector"].count)
            st.rel_f = (c.sc, P.es["scalar"].count)

        pre(0)
        fqblock(0)
        s1(0)
        for t in range(len(tiles)):
            btf_yh(t)
            if t + 1 < len(tiles):
                pre(t + 1)
                fqblock(t + 1)
            wout(t)
            if t + 1 < len(tiles):
                s1(t + 1)
        P.drain(c.rings)
        with nc.Block() as blk:
            P.realize(blk)


def phase_ffn2(nc, c, D, NT):
    with ExitStack() as stp:
        sbp = lambda n, s, d: stp.enter_context(nc.sbuf_tensor(n, s, d))
        psp = lambda n, s: stp.enter_context(nc.psum_tensor(n, s, F32))
        c.sq8 = sbp("sq83", [128, 2, 512], F32)
        c.xln = sbp("xln3", [128, NB_DM, 1028], BF16)
        c.h_ffn = sbp("h_ffn3", [128, NB_HF, 1028], BF16)
        c.wbufA = sbp("wbufA3", [128, 2, DM], BF16)
        c.wbufB = sbp("wbufB3", [128, 2, DM], BF16)
        c.wdbuf = sbp("wdbuf3", [128, 2, HF], BF16)
        c.p_sum = psp("p_sum3", [1, 512])
        c.p_sq = psp("p_sq3", [1, 512])
        c.p_g = psp("p_g3", [128, 2, 512])
        c.p_u = psp("p_u3", [128, 2, 512])
        c.p_d = psp("p_d3", [128, 2, 512])
        c.p_bc1 = c.p_g[:, 0, :]
        c.p_bc2 = c.p_g[:, 1, :]
        c.rA = Ring(2, c.rings[6:8])
        c.rB = Ring(2, c.rings[8:10])
        c.rD = Ring(2, c.rings[10:12])
        P = Prog(nc, c.esems, c.lw)
        c.resid_ready = (c.ve, c.esems["vector"].count)
        c.rel_sq = c.rel_ps = c.rel_bc = None
        c.rel_sq2 = [None, None]
        subs = subtiles(NT)
        emit_ffn(P, c, split_supers(subs, 1028), D.wgq, D.wuq, D.wdq)
        for b in range(NB_DM):
            P.op("sync", lambda e, b=b: e.dma_start(
                D.out[bass.ts(b, 128), :], c.resid[:, b, :]),
                reads=[c.resid_ready], dma=c.qst)
        P.waits("sync", [c.qst])
        with nc.Block() as blk:
            P.realize(blk)


def emit_collective(nc, c, kind, in_ap, out_ap, pre_waits, ncores=NCORES):
    P = Prog(nc, c.esems, c.lw)
    P.waits("gpsimd", pre_waits)

    def build(e):
        with allow_ici_all_to_all():
            return e.collective_compute(
                kind, ALU.bypass, replica_groups=[list(range(ncores))],
                ins=[in_ap.opt()], outs=[out_ap.opt()])
    P.op("gpsimd", build)
    done = (c.esems["gpsimd"], c.esems["gpsimd"].count)
    with nc.Block() as blk:
        P.realize(blk)
    return done


def _l1_dram(nc, nt_own):
    NT = nt_own + HALO
    D = D_()
    D.x = nc.dram_tensor("x_fm", [DM, NT], F32, kind="ExternalInput").ap()
    D.win = nc.dram_tensor("win_sw", [NB_XZ, 128, DM], F32, kind="ExternalInput").ap()
    D.wx = nc.dram_tensor("wx_sw", [NB_DI, 128, 96], F32, kind="ExternalInput").ap()
    D.wdt = nc.dram_tensor("wdt_sw", [NB_DI, 64, 128], F32, kind="ExternalInput").ap()
    D.wg = nc.dram_tensor("wg_sw", [NB_HF, 128, DM], I32, kind="ExternalInput").ap()
    D.wu = nc.dram_tensor("wu_sw", [NB_HF, 128, DM], I32, kind="ExternalInput").ap()
    D.wd = nc.dram_tensor("wd_sw", [NB_DM, 128, HF], I32, kind="ExternalInput").ap()
    D.cw = nc.dram_tensor("cw_fm", [128, NB_DI * DC], F32, kind="ExternalInput").ap()
    D.cbn = nc.dram_tensor("cb_fm", [128, NB_DI], F32, kind="ExternalInput").ap()
    D.oc = nc.dram_tensor("ones_col", [128, 1], F32, kind="ExternalInput").ap()
    D.orr = nc.dram_tensor("ones_row", [1, 128], F32, kind="ExternalInput").ap()
    D.wgq = nc.dram_tensor("wg_q", [NB_HF, 128, DM], BF16).ap()
    D.wuq = nc.dram_tensor("wu_q", [NB_HF, 128, DM], BF16).ap()
    D.wdq = nc.dram_tensor("wd_q", [NB_DM, 128, HF], BF16).ap()
    D.winq = nc.dram_tensor("win_q", [NB_XZ, 128, DM], BF16).ap()
    return D


def _alloc_stage(nc, st, c, tag=""):
    c.stage = st.enter_context(nc.sbuf_tensor("stage" + tag, [128, 3584], F32))
    c.lnrow = st.enter_context(nc.sbuf_tensor("lnrow" + tag, [1, 2048], F32))


def _alloc_l1_persist(nc, st, c, NT):
    c.resid = st.enter_context(nc.sbuf_tensor("resid", [128, NB_DM, NT], F32))
    c.wxq = st.enter_context(nc.sbuf_tensor("wxq", [128, NB_DI, 96], BF16))
    c.wdtq = st.enter_context(nc.sbuf_tensor("wdtq", [64, NB_DI, 128], BF16))
    c.taps = st.enter_context(nc.sbuf_tensor("taps", [128, NB_DI * DC], F32))
    c.cb = st.enter_context(nc.sbuf_tensor("cb", [128, NB_DI], F32))


def build_l1(nt_own):
    NT = nt_own + HALO
    nc = bass.Bass("TRN2", target_bir_lowering=False)
    D = _l1_dram(nc, nt_own)
    d_x1 = nc.dram_tensor("x1_fm", [DM, nt_own], F32, kind="ExternalOutput").ap()
    d_z = nc.dram_tensor("z_codes", [DI, nt_own], BF16, kind="ExternalOutput").ap()
    d_u = nc.dram_tensor("u_codes", [DI, nt_own], BF16, kind="ExternalOutput").ap()
    d_dtq = nc.dram_tensor("dtq_codes", [DI, nt_own], BF16, kind="ExternalOutput").ap()
    d_B = nc.dram_tensor("B_codes", [DS, nt_own], BF16, kind="ExternalOutput").ap()
    d_C = nc.dram_tensor("C_codes", [DS, nt_own], BF16, kind="ExternalOutput").ap()
    D.x1_dst = lambda b: d_x1[bass.ts(b, 128), :]
    D.z_dst = lambda blk, oo, w: d_z[bass.ts(blk, 128), oo:oo + w]
    D.u_dst = lambda b, oo, w: d_u[bass.ts(b, 128), oo:oo + w]
    D.dtq_dst = lambda dc, oo, w: d_dtq[bass.ts(dc, 128), oo:oo + w]
    D.B_dst = lambda oo, w: d_B[:, oo:oo + w]
    D.C_dst = lambda oo, w: d_C[:, oo:oo + w]
    subs_all = subtiles(NT, first=HALO)
    with ExitStack() as st0:
        c = new_ctx(nc, st0)
        _alloc_stage(nc, st0, c)
        _alloc_l1_persist(nc, st0, c, NT)
        make_sems(nc, st0, c)
        with ExitStack() as stq:
            p0_alloc(nc, stq, c)
            phase_p0(nc, c, D)
            phase_ffn1(nc, c, D, NT, subs_all,
                       steps=p0_quant_steps(nc, c, D))
        phase_mamba_in(nc, c, D, NT, subs_all)
    return nc


def build_l2(seq_len, nb_lane):
    nc = bass.Bass("TRN2", target_bir_lowering=False)
    LD = nb_lane * 128
    d_u = nc.dram_tensor("u_lane", [LD, seq_len], BF16, kind="ExternalInput").ap()
    d_dt = nc.dram_tensor("dtq_lane", [LD, seq_len], BF16, kind="ExternalInput").ap()
    d_b = nc.dram_tensor("bcode", [DS, seq_len], BF16, kind="ExternalInput").ap()
    d_cc = nc.dram_tensor("ccode", [DS, seq_len], BF16, kind="ExternalInput").ap()
    D = D_()
    D.al = nc.dram_tensor("alog_fm", [128, nb_lane * DS], F32, kind="ExternalInput").ap()
    D.dtb = nc.dram_tensor("dtb_fm", [128, nb_lane], F32, kind="ExternalInput").ap()
    D.dD = nc.dram_tensor("dD_fm", [128, nb_lane], F32, kind="ExternalInput").ap()
    d_y = nc.dram_tensor("y_lane", [LD, seq_len], F32, kind="ExternalOutput").ap()
    TC = SC_TC
    D.u_src = lambda db, t0: d_u[bass.ts(db, 128), t0:t0 + TC]
    D.dt_src = lambda db, t0: d_dt[bass.ts(db, 128), t0:t0 + TC]
    D.b_src = lambda t0: d_b[:, t0:t0 + TC]
    D.c_src = lambda t0: d_cc[:, t0:t0 + TC]
    D.y_dst = lambda db, t0: d_y[bass.ts(db, 128), t0:t0 + TC]
    with ExitStack() as st0:
        c = Ctx()
        make_sems(nc, st0, c)
        phase_scan(nc, c, D, nb_lane, seq_len)
    return nc


def _l3_dram(nc, nt_own):
    D = D_()
    D.wo = nc.dram_tensor("wo_sw", [NB_DI, 128, DM], F32, kind="ExternalInput").ap()
    D.h128b = nc.dram_tensor("h128b", [128, 128], BF16, kind="ExternalInput").ap()
    D.h128f = nc.dram_tensor("h128f", [128, 128], F32, kind="ExternalInput").ap()
    D.out = nc.dram_tensor("out_fm", [DM, nt_own], F32, kind="ExternalOutput").ap()
    return D


def build_l3(nt_own):
    NT = nt_own
    nc = bass.Bass("TRN2", target_bir_lowering=False)
    D = _l3_dram(nc, nt_own)
    d_x1 = nc.dram_tensor("x1_fm", [DM, NT], F32, kind="ExternalInput").ap()
    d_y = nc.dram_tensor("y_tok", [DI, NT], F32, kind="ExternalInput").ap()
    d_z = nc.dram_tensor("z_codes", [DI, NT], BF16, kind="ExternalInput").ap()
    D.wg = nc.dram_tensor("wg_sw", [NB_HF, 128, DM], I32, kind="ExternalInput").ap()
    D.wu = nc.dram_tensor("wu_sw", [NB_HF, 128, DM], I32, kind="ExternalInput").ap()
    D.wd = nc.dram_tensor("wd_sw", [NB_DM, 128, HF], I32, kind="ExternalInput").ap()
    D.oc = nc.dram_tensor("ones_col", [128, 1], F32, kind="ExternalInput").ap()
    D.orr = nc.dram_tensor("ones_row", [1, 128], F32, kind="ExternalInput").ap()
    D.wgq = nc.dram_tensor("wg_q3", [NB_HF, 128, DM], BF16).ap()
    D.wuq = nc.dram_tensor("wu_q3", [NB_HF, 128, DM], BF16).ap()
    D.wdq = nc.dram_tensor("wd_q3", [NB_DM, 128, HF], BF16).ap()
    D.x1_src = lambda b: d_x1[bass.ts(b, 128), :]
    D.y_src = lambda off, w: d_y.rearrange("(b p) t -> p b t", p=128)[:, :, off:off + w]
    D.z_src = lambda off, w: d_z.rearrange("(b p) t -> p b t", p=128)[:, :, off:off + w]
    with ExitStack() as st0:
        c = new_ctx(nc, st0)
        _alloc_stage(nc, st0, c)
        c.resid = st0.enter_context(
            nc.sbuf_tensor("resid3", [128, NB_DM, NT], F32))
        h128b = st0.enter_context(nc.sbuf_tensor("h128bs", [128, 128], BF16))
        h128f = st0.enter_context(nc.sbuf_tensor("h128fs", [128, 128], F32))
        make_sems(nc, st0, c)
        with ExitStack() as stq:
            whq = stq.enter_context(
                nc.sbuf_tensor("whq", [128, NB_DI, DM], BF16))
            phase_wout_prep(nc, c, D, whq, h128b, h128f, load_consts=True)
            phase_mix(nc, c, D, NT, whq, h128b)
        phase_ffn2(nc, c, D, NT)
    return nc


def build_fused(nt_own, ncores):
    NT = nt_own + HALO
    nb_lane = NB_DI // ncores
    LD = nb_lane * 128
    seq_len = nt_own * ncores
    nc = bass.Bass("TRN2", target_bir_lowering=False)
    D = _l1_dram(nc, nt_own)
    D3 = _l3_dram(nc, nt_own)
    D3.wgq, D3.wuq, D3.wdq = D.wgq, D.wuq, D.wdq
    D2 = D_()
    D2.al = nc.dram_tensor("alog_fm", [128, nb_lane * DS], F32, kind="ExternalInput").ap()
    D2.dtb = nc.dram_tensor("dtb_fm", [128, nb_lane], F32, kind="ExternalInput").ap()
    D2.dD = nc.dram_tensor("dD_fm", [128, nb_lane], F32, kind="ExternalInput").ap()
    r_spill = nc.dram_tensor("r_spill", [DM, nt_own], F32).ap()
    z_scr = nc.dram_tensor("z_scr", [DI, nt_own], BF16).ap()
    cc1_in = nc.dram_tensor("cc1_in", [ncores, 2, LD, nt_own], BF16).ap()
    cc1_out = nc.dram_tensor("cc1_out", [ncores, 2, LD, nt_own], BF16).ap()
    ccg_in = nc.dram_tensor("ccg_in", [2, DS, nt_own], BF16).ap()
    ccg_out = nc.dram_tensor("ccg_out", [ncores, 2, DS, nt_own], BF16).ap()
    cc2_in = nc.dram_tensor("cc2_in", [ncores, nb_lane, 128, nt_own], F32).ap()
    cc2_out = nc.dram_tensor("cc2_out", [ncores, nb_lane, 128, nt_own], F32).ap()

    D.x1_dst = lambda b: r_spill[bass.ts(b, 128), :]
    D.z_dst = lambda blk, oo, w: z_scr[bass.ts(blk, 128), oo:oo + w]
    D.u_dst = lambda b, oo, w: cc1_in[b // nb_lane, 0,
                                      (b % nb_lane) * 128:(b % nb_lane) * 128 + 128,
                                      oo:oo + w]
    D.dtq_dst = lambda dc, oo, w: cc1_in[dc // nb_lane, 1,
                                         (dc % nb_lane) * 128:(dc % nb_lane) * 128 + 128,
                                         oo:oo + w]
    D.B_dst = lambda oo, w: ccg_in[0, :, oo:oo + w]
    D.C_dst = lambda oo, w: ccg_in[1, :, oo:oo + w]
    TC = SC_TC
    D2.u_src = lambda db, t0: cc1_out[t0 // nt_own, 0,
                                      db * 128:db * 128 + 128,
                                      t0 % nt_own:t0 % nt_own + TC]
    D2.dt_src = lambda db, t0: cc1_out[t0 // nt_own, 1,
                                       db * 128:db * 128 + 128,
                                       t0 % nt_own:t0 % nt_own + TC]
    D2.b_src = lambda t0: ccg_out[t0 // nt_own, 0, :,
                                  t0 % nt_own:t0 % nt_own + TC]
    D2.c_src = lambda t0: ccg_out[t0 // nt_own, 1, :,
                                  t0 % nt_own:t0 % nt_own + TC]
    D2.y_dst = lambda db, t0: cc2_in[t0 // nt_own, db, :,
                                     t0 % nt_own:t0 % nt_own + TC]
    D3.x1_src = lambda b: r_spill[bass.ts(b, 128), :]
    D3.y_src = lambda off, w: cc2_out.rearrange(
        "k l p t -> p (k l) t")[:, :, off:off + w]
    D3.z_src = lambda off, w: z_scr.rearrange(
        "(b p) t -> p b t", p=128)[:, :, off:off + w]

    subs_all = subtiles(NT, first=HALO)
    with ExitStack() as st0:
        c = new_ctx(nc, st0)
        make_sems(nc, st0, c)
        with ExitStack() as stA:
            _alloc_stage(nc, stA, c, "A")
            _alloc_l1_persist(nc, stA, c, NT)
            with ExitStack() as stq:
                p0_alloc(nc, stq, c)
                phase_p0(nc, c, D)
                phase_ffn1(nc, c, D, NT, subs_all,
                           steps=p0_quant_steps(nc, c, D))
            phase_mamba_in(nc, c, D, NT, subs_all)
        cc1_done = emit_collective(nc, c, "AllToAll", cc1_in, cc1_out,
                                   [c.qust, c.swrings2[2], c.swrings2[3]],
                                   ncores)
        ccg_done = emit_collective(nc, c, "AllGather", ccg_in, ccg_out,
                                   [c.qbc], ncores)
        with ExitStack() as stB:
            Pw = Prog(nc, c.esems, c.lw)
            Pw.waits("sync", [cc1_done, ccg_done])
            Pw.waits("scalar", [cc1_done, ccg_done])
            with nc.Block() as blk:
                Pw.realize(blk)
            phase_scan(nc, c, D2, nb_lane, seq_len)
        cc2_done = emit_collective(nc, c, "AllToAll", cc2_in, cc2_out,
                                   c.swrings + c.swrings2[0:2], ncores)
        with ExitStack() as stC:
            _alloc_stage(nc, stC, c, "C")
            c.resid = stC.enter_context(
                nc.sbuf_tensor("resid3", [128, NB_DM, nt_own], F32))
            h128b = stC.enter_context(nc.sbuf_tensor("h128bs", [128, 128], BF16))
            h128f = stC.enter_context(nc.sbuf_tensor("h128fs", [128, 128], F32))
            with ExitStack() as stq:
                whq = stq.enter_context(
                    nc.sbuf_tensor("whq", [128, NB_DI, DM], BF16))
                phase_wout_prep(nc, c, D3, whq, h128b, h128f,
                                load_consts=False)
                Pw = Prog(nc, c.esems, c.lw)
                Pw.waits("sync", [cc2_done])
                Pw.waits("scalar", [cc2_done])
                with nc.Block() as blk:
                    Pw.realize(blk)
                phase_mix(nc, c, D3, nt_own, whq, h128b)
            phase_ffn2(nc, c, D3, nt_own)
    return nc


def stage_weights(z):
    f32 = np.float32
    wgT = np.ascontiguousarray(z['L0_ffn_w_gate'].T)
    wuT = np.ascontiguousarray(z['L0_ffn_w_up'].T)
    wdT = np.ascontiguousarray(z['L0_ffn_w_down'].T)
    wg_sw = np.ascontiguousarray(
        wgT.reshape(NB_DM, 128, NB_HF, 128).transpose(2, 1, 0, 3).reshape(NB_HF, 128, DM))
    wu_sw = np.ascontiguousarray(
        wuT.reshape(NB_DM, 128, NB_HF, 128).transpose(2, 1, 0, 3).reshape(NB_HF, 128, DM))
    wd_sw = np.ascontiguousarray(
        wdT.reshape(NB_HF, 128, NB_DM, 128).transpose(2, 1, 0, 3).reshape(NB_DM, 128, HF))
    winT = np.ascontiguousarray(z['L0_attn_W_in'].T.astype(f32))
    win_sw = np.ascontiguousarray(
        winT.reshape(NB_DM, 128, NB_XZ, 128).transpose(2, 1, 0, 3).reshape(NB_XZ, 128, DM))
    wx_sw = np.ascontiguousarray(z['L0_attn_W_x'].T.astype(f32).reshape(NB_DI, 128, 96))
    wdt_sw = np.ascontiguousarray(
        z['L0_attn_W_dt'].T.astype(f32).reshape(64, NB_DI, 128).transpose(1, 0, 2))
    cw = z['L0_attn_conv_w'].astype(f32)
    cw_fm = np.ascontiguousarray(
        cw.reshape(NB_DI, 128, DC).transpose(1, 0, 2).reshape(128, NB_DI * DC))
    cb_fm = np.ascontiguousarray(z['L0_attn_conv_b'].astype(f32).reshape(NB_DI, 128).T)
    wo_sw = np.ascontiguousarray(z['L0_attn_W_out'].T.astype(f32).reshape(NB_DI, 128, DM))
    i = np.arange(128)
    H128 = np.where((np.bitwise_count(i[:, None] & i[None, :]) % 2) == 0,
                    1.0, -1.0).astype(f32)
    return dict(wg_sw=wg_sw, wu_sw=wu_sw, wd_sw=wd_sw, win_sw=win_sw,
                wx_sw=wx_sw, wdt_sw=wdt_sw, cw_fm=cw_fm, cb_fm=cb_fm,
                wo_sw=wo_sw, h128f=H128, h128b=H128.astype(BF),
                ones_col=np.ones((128, 1), f32), ones_row=np.ones((1, 128), f32))


def _lane_params(z, j, nbl):
    f32 = np.float32
    sl = slice(j * nbl * 128, (j + 1) * nbl * 128)
    al = z['L0_attn_A_log'].astype(f32)[sl].reshape(nbl, 128, DS)
    return dict(
        alog_fm=np.ascontiguousarray(al.transpose(1, 0, 2).reshape(128, nbl * DS)),
        dtb_fm=np.ascontiguousarray(
            z['L0_attn_dt_bias'].astype(f32)[sl].reshape(nbl, 128).T),
        dD_fm=np.ascontiguousarray(
            z['L0_attn_D'].astype(f32)[sl].reshape(nbl, 128).T))


def stage_l1(x_tokens, W, ncores, nt_own):
    maps = []
    for k in range(ncores):
        lo = k * nt_own - HALO
        xs = np.zeros((nt_own + HALO, DM), np.float32)
        a = max(0, -lo)
        xs[a:] = x_tokens[lo + a:lo + nt_own + HALO]
        maps.append(dict(
            x_fm=np.ascontiguousarray(xs.T),
            win_sw=W['win_sw'], wx_sw=W['wx_sw'], wdt_sw=W['wdt_sw'],
            wg_sw=W['wg_sw'], wu_sw=W['wu_sw'], wd_sw=W['wd_sw'],
            cw_fm=W['cw_fm'], cb_fm=W['cb_fm'],
            ones_col=W['ones_col'], ones_row=W['ones_row']))
    return maps


def stage_l2(r1, z, ncores, nt_own):
    u_all = np.concatenate([np.asarray(r['u_codes']) for r in r1], axis=1)
    dt_all = np.concatenate([np.asarray(r['dtq_codes']) for r in r1], axis=1)
    B_all = np.concatenate([np.asarray(r['B_codes']) for r in r1], axis=1)
    C_all = np.concatenate([np.asarray(r['C_codes']) for r in r1], axis=1)
    nbl = NB_DI // ncores
    ld = nbl * 128
    maps = []
    for j in range(ncores):
        sl = slice(j * ld, (j + 1) * ld)
        m = dict(
            u_lane=np.ascontiguousarray(u_all[sl]),
            dtq_lane=np.ascontiguousarray(dt_all[sl]),
            bcode=np.ascontiguousarray(B_all),
            ccode=np.ascontiguousarray(C_all))
        m.update(_lane_params(z, j, nbl))
        maps.append(m)
    return maps


def stage_l3(r1, r2, W, ncores, nt_own):
    y_all = np.concatenate([np.asarray(r['y_lane']) for r in r2], axis=0)
    maps = []
    for k in range(ncores):
        sl = slice(k * nt_own, (k + 1) * nt_own)
        maps.append(dict(
            x1_fm=np.asarray(r1[k]['x1_fm']),
            y_tok=np.ascontiguousarray(y_all[:, sl]),
            z_codes=np.asarray(r1[k]['z_codes']),
            wg_sw=W['wg_sw'], wu_sw=W['wu_sw'], wd_sw=W['wd_sw'],
            wo_sw=W['wo_sw'], h128b=W['h128b'], h128f=W['h128f'],
            ones_col=W['ones_col'], ones_row=W['ones_row']))
    return maps


def stage_fused(x_tokens, W, z, ncores, nt_own):
    maps = stage_l1(x_tokens, W, ncores, nt_own)
    nbl = NB_DI // ncores
    for k in range(ncores):
        maps[k].update(_lane_params(z, k, nbl))
        maps[k].update(wo_sw=W['wo_sw'], h128b=W['h128b'], h128f=W['h128f'])
    return maps


def kernel(**inputs):
    z = inputs
    x = np.asarray(z['L0_attn_hidden_states'][0], np.float32)
    W = stage_weights(z)
    cores = list(range(NCORES))
    if USE_FUSED:
        nc = build_fused(NT_OWN, NCORES)
        r = run_bass_kernel_spmd(nc, stage_fused(x, W, z, NCORES, NT_OWN),
                                 core_ids=cores).results
        out = np.concatenate([np.asarray(rr['out_fm']).T for rr in r], axis=0)
        return out[None].astype(np.float32)
    l1 = build_l1(NT_OWN)
    r1 = run_bass_kernel_spmd(l1, stage_l1(x, W, NCORES, NT_OWN),
                              core_ids=cores).results
    l2 = build_l2(SEQ, NB_DI // NCORES)
    r2 = run_bass_kernel_spmd(l2, stage_l2(r1, z, NCORES, NT_OWN),
                              core_ids=cores).results
    l3 = build_l3(NT_OWN)
    r3 = run_bass_kernel_spmd(l3, stage_l3(r1, r2, W, NCORES, NT_OWN),
                              core_ids=cores).results
    out = np.concatenate([np.asarray(r['out_fm']).T for r in r3], axis=0)
    return out[None].astype(np.float32)


# version 24 (final)
# speedup vs baseline: 1.0317x; 1.0317x over previous
import numpy as np
import ml_dtypes
from contextlib import ExitStack

import concourse.bass as bass
import concourse.mybir as mybir
from concourse.bass_utils import run_bass_kernel_spmd
from concourse.aot_env import allow_ici_all_to_all

F32 = mybir.dt.float32
BF16 = mybir.dt.bfloat16
I32 = mybir.dt.int32
AF = mybir.ActivationFunctionType
ALU = mybir.AluOpType
AX = mybir.AxisListType
BF = ml_dtypes.bfloat16

DM, DI, DXZ, HF, DS, DC, DR = 1024, 2048, 4096, 3072, 16, 4, 64
NCORES, SEQ = 8, 16384
S_IN, S_XZ, S_CONV, S_XDBL, S_DT, S_Y = 0.05, 0.06, 0.04, 0.05, 0.05, 0.1
CM = float(1.5 * 2 ** 23)
EPS = 1e-5
NB_DM, NB_DI, NB_XZ, NB_HF = DM // 128, DI // 128, DXZ // 128, HF // 128
HALO = 4
NT_OWN = SEQ // NCORES
SC_TC = 512
DEBUG_L1 = False
USE_FUSED = True


class Sem:
    def __init__(self, handle, name):
        self.h, self.name, self.count = handle, name, 0


class Prog:
    ENGS = ("sync", "scalar", "vector", "tensor", "gpsimd")

    def __init__(self, nc, esems, last_wait):
        self.nc = nc
        self.streams = {e: [] for e in self.ENGS}
        self.es = esems
        self._lw = last_wait

    def _wl(self, eng, reads):
        wl = []
        for w in reads:
            if w is None:
                continue
            s, cnt = (w, w.count) if isinstance(w, Sem) else w
            if cnt is None or cnt <= 0:
                continue
            key = (eng, s.name)
            if self._lw.get(key, -1) >= cnt:
                continue
            self._lw[key] = cnt
            wl.append((s.h, cnt))
        return wl

    def op(self, eng, build, reads=(), dma=None):
        wl = self._wl(eng, reads)
        sem, n = (dma, 16) if dma is not None else (self.es[eng], 1)
        sem.count += n

        def run(e, wl=wl, build=build, sem=sem, n=n):
            for h, cnt in wl:
                e.wait_ge(h, cnt)
            build(e).then_inc(sem.h, n)
        self.streams[eng].append(run)
        return (sem, sem.count)

    def waits(self, eng, reads):
        wl = self._wl(eng, reads)

        def run(e, wl=wl):
            for h, cnt in wl:
                e.wait_ge(h, cnt)
        self.streams[eng].append(run)

    def drain(self, sems):
        lst = list(sems)
        for eng in ("sync", "scalar", "gpsimd"):
            self.waits(eng, lst)

    def realize(self, block):
        for name in self.ENGS:
            stream = self.streams[name]
            if stream:
                getattr(block, name)(lambda e, s=stream: [fn(e) for fn in s])


class Ring:
    def __init__(self, n, sems):
        self.n, self.sems = n, sems
        self.free_at = [None] * n
        self.i = 0

    def acquire(self):
        j = self.i % self.n
        self.i += 1
        return j, self.free_at[j]

    def release(self, j, semcount):
        self.free_at[j] = semcount


def subtiles(total, first=None):
    out, off = [], 0
    if first:
        out.append((0, first))
        off = first
    while off < total:
        w = min(512, total - off)
        out.append((off, w))
        off += w
    return out


def split_supers(subs, maxw=1028):
    sup, cur, wsum = [], [], 0
    for (off, w) in subs:
        if cur and wsum + w > maxw:
            sup.append(cur)
            cur, wsum = [], 0
        cur.append((off, w))
        wsum += w
    if cur:
        sup.append(cur)
    return sup


class Ctx:
    pass


class D_:
    pass


def emit_fq3(P, c, dst_ap, src_ap, w, scale, extra_reads=(), parts=128):
    P.op("scalar", lambda e: e.activation(
        c.stage[0:parts, 0:w], src_ap, AF.Identity,
        bias=c.cmag[0:parts, 0:1], scale=scale),
        reads=list(extra_reads) + [c.te, c.sc, c.ve])
    rel_src = (c.sc, P.es["scalar"].count)
    P.op("vector", lambda e: e.tensor_scalar(
        c.stage[0:parts, 512:512 + w], c.stage[0:parts, 0:w], CM, 127.0,
        ALU.subtract, ALU.min), reads=[c.sc, c.ve])
    P.op("vector", lambda e: e.tensor_scalar(
        dst_ap, c.stage[0:parts, 512:512 + w], -128.0, None, ALU.max),
        reads=[c.ve])
    return rel_src


def emit_ln(P, c, subs, apply_fn, fq_scale=None):
    R = c.lnrow
    for (off, w) in subs:
        for b in range(NB_DM):
            P.op("tensor", lambda e, b=b, off=off, w=w: e.matmul(
                c.p_sum[0:1, 0:w], c.ones_c[:], c.resid[:, b, off:off + w],
                start=(b == 0), stop=(b == NB_DM - 1)),
                reads=[c.resid_ready, c.q1, c.rel_ps])
        for b in range(NB_DM):
            js = b % 2
            sqr = P.op("scalar", lambda e, b=b, js=js, off=off, w=w: e.square(
                c.sq8[:, js, 0:w], c.resid[:, b, off:off + w]),
                reads=[c.resid_ready, c.rel_sq2[js], c.sc])
            P.op("tensor", lambda e, b=b, js=js, w=w: e.matmul(
                c.p_sq[0:1, 0:w], c.ones_c[:], c.sq8[:, js, 0:w],
                start=(b == 0), stop=(b == NB_DM - 1)), reads=[sqr])
            c.rel_sq2[js] = (c.te, P.es["tensor"].count)
        c.rel_sq = (c.te, P.es["tensor"].count)
        P.op("vector", lambda e, w=w: e.tensor_scalar(
            R[0:1, 0:w], c.p_sum[0:1, 0:w], 1.0 / DM, None, ALU.mult),
            reads=[c.te, c.ve])
        P.op("vector", lambda e, w=w: e.tensor_scalar(
            R[0:1, 512:512 + w], c.p_sq[0:1, 0:w], 1.0 / DM, None, ALU.mult),
            reads=[c.ve])
        c.rel_ps = (c.ve, P.es["vector"].count)
        P.op("vector", lambda e, w=w: e.tensor_tensor(
            R[0:1, 1024:1024 + w], R[0:1, 0:w], R[0:1, 0:w], ALU.mult),
            reads=[c.ve])
        P.op("vector", lambda e, w=w: e.tensor_tensor(
            R[0:1, 1024:1024 + w], R[0:1, 512:512 + w], R[0:1, 1024:1024 + w],
            ALU.subtract), reads=[c.ve])
        P.op("scalar", lambda e, w=w: e.activation(
            R[0:1, 1536:1536 + w], R[0:1, 1024:1024 + w], AF.Ln,
            bias=c.eps[0:1, 0:1]), reads=[c.ve, c.sc])
        P.op("scalar", lambda e, w=w: e.activation(
            R[0:1, 1536:1536 + w], R[0:1, 1536:1536 + w], AF.Exp,
            scale=-0.5), reads=[c.sc])
        if fq_scale is not None:
            P.op("vector", lambda e, w=w: e.tensor_scalar(
                R[0:1, 1536:1536 + w], R[0:1, 1536:1536 + w], float(fq_scale),
                None, ALU.mult), reads=[c.sc, c.ve])
        P.op("vector", lambda e, w=w: e.tensor_tensor(
            R[0:1, 1024:1024 + w], R[0:1, 0:w], R[0:1, 1536:1536 + w],
            ALU.mult), reads=[c.sc, c.ve])
        P.op("tensor", lambda e, w=w: e.matmul(
            c.p_bc1[:, 0:w], c.ones_r[:], R[0:1, 1536:1536 + w],
            start=True, stop=True), reads=[c.ve, c.sc, c.q1, c.rel_bc])
        P.op("tensor", lambda e, w=w: e.matmul(
            c.p_bc2[:, 0:w], c.ones_r[:], R[0:1, 1024:1024 + w],
            start=True, stop=True), reads=[c.ve])
        for b in range(NB_DM):
            apply_fn(b, off, w)
        c.rel_bc = (c.ve, P.es["vector"].count)


def emit_ffn(P, c, supers, wgq, wuq, wdq, steps=()):
    steps = list(steps)
    tick = lambda: steps and steps.pop(0)(P)

    def do_ln(sup):
        off0 = sup[0][0]

        def apply(b, off, w, off0=off0):
            P.op("vector", lambda e, b=b, off=off, w=w: e.tensor_tensor(
                c.stage[:, 1024:1024 + w], c.resid[:, b, off:off + w],
                c.p_bc1[:, 0:w], ALU.mult), reads=[c.te, c.ve])
            P.op("vector", lambda e, b=b, off=off, w=w, off0=off0: e.tensor_tensor(
                c.xln[:, b, off - off0:off - off0 + w],
                c.stage[:, 1024:1024 + w], c.p_bc2[:, 0:w], ALU.subtract),
                reads=[c.ve])
        emit_ln(P, c, sup, apply)
        return (c.ve, P.es["vector"].count)

    nxt_xd = do_ln(supers[0])
    for si_, sup in enumerate(supers):
        off0 = sup[0][0]
        xln_done = nxt_xd
        tick()

        rel_pg2 = [c.rel_pg, c.rel_pg]
        rel_pu2 = [c.rel_pu, c.rel_pu]
        sstage = [2048, 3072]
        rel_ss = [None, None]
        for hc in range(NB_HF):
            jp = hc % 2
            ja, relA = c.rA.acquire()
            P.op("sync", lambda e, hc=hc, ja=ja: e.dma_start(
                c.wbufA[:, ja, :], wgq[hc]),
                reads=[relA, getattr(c, "m_wg", None)], dma=c.rA.sems[ja])
            ldA = (c.rA.sems[ja], c.rA.sems[ja].count)
            jb, relB = c.rB.acquire()
            P.op("scalar", lambda e, hc=hc, jb=jb: e.dma_start(
                c.wbufB[:, jb, :], wuq[hc]),
                reads=[relB, getattr(c, "m_wu", None)], dma=c.rB.sems[jb])
            ldB = (c.rB.sems[jb], c.rB.sems[jb].count)
            for (off, w) in sup:
                o2 = off - off0
                for dc in range(NB_DM):
                    P.op("tensor", lambda e, dc=dc, o2=o2, w=w, ja=ja, jp=jp: e.matmul(
                        c.p_g[:, jp, 0:w], c.wbufA[:, ja, dc * 128:(dc + 1) * 128],
                        c.xln[:, dc, o2:o2 + w], start=(dc == 0),
                        stop=(dc == NB_DM - 1)), reads=[ldA, xln_done, rel_pg2[jp]])
                for dc in range(NB_DM):
                    P.op("tensor", lambda e, dc=dc, o2=o2, w=w, jb=jb, jp=jp: e.matmul(
                        c.p_u[:, jp, 0:w], c.wbufB[:, jb, dc * 128:(dc + 1) * 128],
                        c.xln[:, dc, o2:o2 + w], start=(dc == 0),
                        stop=(dc == NB_DM - 1)), reads=[ldB, rel_pu2[jp]])
                mm_done = (c.te, P.es["tensor"].count)
                P.op("scalar", lambda e, w=w, jp=jp: e.activation(
                    c.stage[:, sstage[jp]:sstage[jp] + w], c.p_g[:, jp, 0:w],
                    AF.Silu, scale=1.0 / 127.0, bias=c.zero[:, 0:1]),
                    reads=[mm_done, c.sc, rel_ss[jp]])
                rel_pg2[jp] = (c.sc, P.es["scalar"].count)
                P.op("vector", lambda e, hc=hc, o2=o2, w=w, jp=jp: e.tensor_tensor(
                    c.h_ffn[:, hc, o2:o2 + w], c.stage[:, sstage[jp]:sstage[jp] + w],
                    c.p_u[:, jp, 0:w], ALU.mult), reads=[rel_pg2[jp], mm_done, c.ve])
                rel_pu2[jp] = (c.ve, P.es["vector"].count)
                rel_ss[jp] = rel_pu2[jp]
            c.rA.release(ja, (c.te, P.es["tensor"].count))
            c.rB.release(jb, (c.te, P.es["tensor"].count))
            tick()
        c.rel_pg = rel_pg2[1]
        c.rel_pu = rel_pu2[1]
        h_done = (c.ve, P.es["vector"].count)
        if si_ + 1 < len(supers):
            nxt_xd = do_ln(supers[si_ + 1])

        sdl = [512, 1536]
        for oc in range(NB_DM):
            jd, relD = c.rD.acquire()
            P.op("sync" if oc % 2 == 0 else "scalar",
                 lambda e, oc=oc, jd=jd: e.dma_start(
                     c.wdbuf[:, jd, :], wdq[oc]),
                 reads=[relD, getattr(c, "m_wd", None)], dma=c.rD.sems[jd])
            ldD = (c.rD.sems[jd], c.rD.sems[jd].count)
            for (off, w) in sup:
                o2 = off - off0
                jp = c.jd2
                for hc in range(NB_HF):
                    P.op("tensor", lambda e, hc=hc, o2=o2, w=w, jd=jd, jp=jp: e.matmul(
                        c.p_d[:, jp, 0:w], c.wdbuf[:, jd, hc * 128:(hc + 1) * 128],
                        c.h_ffn[:, hc, o2:o2 + w], start=(hc == 0),
                        stop=(hc == NB_HF - 1)), reads=[ldD, h_done, c.rel_pd2[jp]])
                P.op("scalar", lambda e, w=w, jp=jp: e.activation(
                    c.stage[:, sdl[jp]:sdl[jp] + w], c.p_d[:, jp, 0:w],
                    AF.Identity, scale=0.5 / (127.0 * 127.0),
                    bias=c.zero[:, 0:1]), reads=[c.te, c.sc, c.rel_sd2[jp]])
                P.op("vector", lambda e, oc=oc, off=off, w=w, jp=jp: e.tensor_tensor(
                    c.resid[:, oc, off:off + w], c.resid[:, oc, off:off + w],
                    c.stage[:, sdl[jp]:sdl[jp] + w], ALU.add),
                    reads=[c.sc, c.ve])
                c.rel_pd2[jp] = (c.ve, P.es["vector"].count)
                c.rel_sd2[jp] = c.rel_pd2[jp]
                c.jd2 = 1 - jp
            c.rD.release(jd, (c.te, P.es["tensor"].count))
            tick()
    while steps:
        steps.pop(0)(P)
    c.resid_ready = (c.ve, P.es["vector"].count)


def quant_weight(P, c, raw_ap, n, recip_col, codes_dst, extra=()):
    P.op("scalar", lambda e: e.activation(
        c.stage[:, 0:n], raw_ap, AF.Identity, bias=c.cmag[:, 0:1],
        scale=recip_col), reads=list(extra) + [c.sc, c.ve, c.q1])
    P.op("vector", lambda e: e.tensor_scalar(
        c.stage[:, 2048:2048 + n], c.stage[:, 0:n], CM, 127.0,
        ALU.subtract, ALU.min), reads=[c.sc, c.ve])
    P.op("vector", lambda e: e.tensor_scalar(
        codes_dst, c.stage[:, 2048:2048 + n], -128.0, None, ALU.max),
        reads=[c.ve])


def emit_colscale(P, c, row_src_ap, dst_col, pslot):
    P.op("gpsimd", lambda e: e.dma_start(
        c.d_cs[0:1, dst_col:dst_col + 1], row_src_ap),
        reads=[c.ve], dma=c.qg2)
    mid = (c.qg2, c.qg2.count)
    P.op("gpsimd", lambda e: e.dma_start(
        c.acol[:, dst_col:dst_col + 1],
        c.d_cs[0:1, dst_col:dst_col + 1].broadcast_to([128, 1])),
        reads=[mid], dma=c.qg2)
    c.rel_pcol = (c.qg2, c.qg2.count)


def emit_colguard(P, c, col):
    P.op("scalar", lambda e: e.copy(
        c.acol[:, col:col + 1], c.acol[:, col:col + 1]),
        reads=[c.rel_pcol, c.sc])


def emit_absmax_global(P, c, acc_col_ap, row_dst):
    P.op("gpsimd", lambda e: e.dma_start(
        c.rowtmp[0:1, 384:512], acc_col_ap, allow_slow_non_contiguous=True),
        reads=[c.ve], dma=c.qg2)
    P.op("vector", lambda e: e.tensor_reduce(
        c.rowtmp[0:1, row_dst:row_dst + 1], c.rowtmp[0:1, 384:512],
        AX.X, ALU.max), reads=[c.qg2, c.ve])


def recip127_row(P, c, src, dst):
    P.op("vector", lambda e: e.reciprocal(
        c.rowtmp[0:1, dst:dst + 1], c.rowtmp[0:1, src:src + 1]), reads=[c.ve])
    P.op("vector", lambda e: e.tensor_scalar(
        c.rowtmp[0:1, dst:dst + 1], c.rowtmp[0:1, dst:dst + 1], 127.0, None,
        ALU.mult), reads=[c.ve])


def new_ctx(nc, st0):
    sb = lambda n, s, d: st0.enter_context(nc.sbuf_tensor(n, s, d))
    c = Ctx()
    c.d_cs = nc.dram_tensor("cs_scr", [1, 16], F32).ap()
    c.ones_c = sb("ones_c", [128, 1], F32)
    c.ones_r = sb("ones_r", [1, 128], F32)
    c.cmag = sb("cmag", [128, 1], F32)
    c.zero = sb("zero", [128, 1], F32)
    c.eps = sb("eps", [128, 1], F32)
    c.acol = sb("acol", [128, 8], F32)
    c.rowtmp = sb("rowtmp", [1, 512], F32)
    c.amax = sb("amax", [128, 2], F32)
    c.rel_sq = c.rel_ps = c.rel_bc = c.rel_pcol = None
    c.rel_sq2 = [None, None]
    c.rel_pg = c.rel_pu = c.rel_pd = c.rel_sg = c.rel_sd = None
    c.rel_pd2 = [None, None]
    c.rel_sd2 = [None, None]
    c.jd2 = 0
    return c


def make_sems(nc, st0, c):
    SEM = lambda n: Sem(st0.enter_context(nc.semaphore(n)), n)
    c.esems = {e: SEM("e_" + e) for e in Prog.ENGS}
    c.sc, c.ve, c.te = c.esems["scalar"], c.esems["vector"], c.esems["tensor"]
    c.q1, c.qg, c.qg2 = SEM("q1"), SEM("qg"), SEM("qg2")
    c.qst, c.qstg = SEM("qst"), SEM("qstg")
    c.qust, c.qbc = SEM("qust"), SEM("qbc")
    c.rings = [SEM(f"rs{i}") for i in range(16)]
    c.swrings = c.rings[14:16]
    c.swrings2 = [SEM(f"rsw{i}") for i in range(16, 20)]
    c.lw = {}


def p0_alloc(nc, stp, c):
    sbp = lambda n, s, d: stp.enter_context(nc.sbuf_tensor(n, s, d))
    c.q_wraw = sbp("wraw", [128, 2, DM], F32)
    c.q_wq16 = sbp("wq16", [128, 2, DM], BF16)
    c.q_cwbuf = sbp("cwbuf", [128, NB_DI * DC], F32)
    c.qscr = sbp("qscr", [128, 1024], F32)


def quant_weight2(P, c, raw_ap, n, recip_col, codes_dst, parts=128, extra=()):
    for k0 in range(0, n, 1024):
        w = min(1024, n - k0)
        P.op("scalar", lambda e, k0=k0, w=w: e.activation(
            c.qscr[0:parts, 0:w], raw_ap[:, k0:k0 + w], AF.Identity,
            bias=c.cmag[0:parts, 0:1], scale=recip_col),
            reads=list(extra) + [c.sc, c.ve, c.q1])
        P.op("vector", lambda e, w=w: e.tensor_scalar(
            c.qscr[0:parts, 0:w], c.qscr[0:parts, 0:w], CM, 127.0,
            ALU.subtract, ALU.min), reads=[c.sc, c.ve])
        P.op("vector", lambda e, k0=k0, w=w: e.tensor_scalar(
            codes_dst[:, k0:k0 + w], c.qscr[0:parts, 0:w], -128.0, None,
            ALU.max), reads=[c.ve])


def phase_p0(nc, c, D, load_x=True):
    with ExitStack() as stp:
        cwbuf = c.q_cwbuf
        P = Prog(nc, c.esems, c.lw)
        P.op("vector", lambda e: e.memset(c.cmag[:], CM))
        P.op("vector", lambda e: e.memset(c.zero[:], 0.0))
        P.op("vector", lambda e: e.memset(c.eps[:], EPS))
        P.op("vector", lambda e: e.memset(c.amax[:, 0:1], 0.0))
        if load_x:
            for b in range(NB_DM):
                P.op("sync", lambda e, b=b: e.dma_start(
                    c.resid[:, b, :], D.x[bass.ts(b, 128), :]), dma=c.q1)
        P.op("sync", lambda e: e.dma_start(c.ones_c[:], D.oc), dma=c.q1)
        P.op("sync", lambda e: e.dma_start(c.ones_r[:], D.orr), dma=c.q1)
        P.op("sync", lambda e: e.dma_start(c.cb[:], D.cbn), dma=c.q1)
        P.op("sync", lambda e: e.dma_start(cwbuf[:], D.cw), dma=c.q1)
        for (src, dst, mark, sem) in (
                (D.wg, D.wgq, "m_wg", c.qg), (D.wu, D.wuq, "m_wu", c.qstg),
                (D.wd, D.wdq, "m_wd", c.qg2)):
            P.op("gpsimd", lambda e, s=src, d=dst: e.dma_start(d, s), dma=sem)
            setattr(c, mark, (sem, sem.count))
        with nc.Block() as blk:
            P.realize(blk)


def p0_quant_steps(nc, c, D):
    wraw, wq16 = c.q_wraw, c.q_wq16
    cwbuf = c.q_cwbuf
    rW = Ring(2, c.swrings2)
    rS = Ring(2, c.swrings)
    lds = {}
    steps = []

    def loadW(P, wc):
        j, rel = rW.acquire()
        P.op("gpsimd", lambda e, wc=wc, j=j: e.dma_start(
            wraw[:, j, :], D.win[wc]), reads=[rel], dma=rW.sems[j])
        lds[wc] = (j, (rW.sems[j], rW.sems[j].count))

    def redA(P, wc):
        j, ld = lds.pop(wc)
        P.op("vector", lambda e, j=j: e.tensor_reduce(
            c.amax[:, 1:2], wraw[:, j, :], AX.X, ALU.max,
            apply_absolute_value=True), reads=[ld, c.ve])
        r2 = P.op("vector", lambda e: e.tensor_tensor(
            c.amax[:, 0:1], c.amax[:, 0:1], c.amax[:, 1:2], ALU.max),
            reads=[c.ve])
        rW.release(j, r2)

    steps.append(lambda P: (loadW(P, 0), loadW(P, 1)))
    for wc in range(NB_XZ):
        def stepA(P, wc=wc):
            redA(P, wc)
            if wc + 2 < NB_XZ:
                loadW(P, wc + 2)
        steps.append(stepA)

    def finA(P):
        emit_absmax_global(P, c, c.amax[:, 0:1], 0)
        recip127_row(P, c, 0, 1)
        P.op("vector", lambda e: e.tensor_scalar(
            c.rowtmp[0:1, 2:3], c.rowtmp[0:1, 0:1],
            float(S_IN / S_XZ / 127.0), None, ALU.mult), reads=[c.ve])
        emit_colscale(P, c, c.rowtmp[0:1, 1:2], 0, 0)
        emit_colscale(P, c, c.rowtmp[0:1, 2:3], 1, 1)
        emit_colguard(P, c, 1)
    steps.append(finA)

    def procB(P, wc):
        j, ld = lds.pop(wc)
        js, rels = rS.acquire()
        P.op("scalar", lambda e, j=j: e.activation(
            c.qscr[:, 0:DM], wraw[:, j, :], AF.Identity,
            bias=c.cmag[:, 0:1], scale=c.acol[:, 0:1]),
            reads=[ld, c.sc, c.ve, rels])
        rW.release(j, (c.sc, P.es["scalar"].count))
        P.op("vector", lambda e: e.tensor_scalar(
            c.qscr[:, 0:DM], c.qscr[:, 0:DM], CM, 127.0,
            ALU.subtract, ALU.min), reads=[c.sc, c.ve])
        P.op("vector", lambda e, js=js: e.tensor_scalar(
            wq16[:, js, :], c.qscr[:, 0:DM], -128.0, None,
            ALU.max), reads=[c.ve])
        P.op("gpsimd", lambda e, wc=wc, js=js: e.dma_start(
            D.winq[wc], wq16[:, js, :]), reads=[c.ve], dma=rS.sems[js])
        rS.release(js, (rS.sems[js], rS.sems[js].count))
        if wc + 2 < NB_XZ:
            loadW(P, wc + 2)
    stepsB = [lambda P: (loadW(P, 0), loadW(P, 1))]
    for wc in range(NB_XZ):
        stepsB.append(lambda P, wc=wc: procB(P, wc))

    wx_pn = D.wx.rearrange("b p n -> p b n")
    wxq_f = c.wxq[:].rearrange("p b n -> p (b n)")
    wdt_pn = D.wdt.rearrange("b p n -> p b n")
    wdtq_f = c.wdtq[:].rearrange("p b n -> p (b n)")

    def ld_wx(P, half):
        P.op("gpsimd", lambda e, half=half: e.dma_start(
            c.qscr[:, 0:768].rearrange("p (b n) -> p b n", n=96),
            wx_pn[:, half * 8:half * 8 + 8, :]), reads=[c.ve, c.sc],
            dma=c.swrings2[0])
        return (c.swrings2[0], c.swrings2[0].count)

    def ld_wdt(P, half):
        P.op("gpsimd", lambda e, half=half: e.dma_start(
            c.qscr[0:64, 0:1024].rearrange("p (b n) -> p b n", n=128),
            wdt_pn[:, half * 8:half * 8 + 8, :]), reads=[c.ve, c.sc],
            dma=c.swrings2[0])
        return (c.swrings2[0], c.swrings2[0].count)

    def stepC1(P, half):
        ld = ld_wx(P, half)
        P.op("vector", lambda e, half=half: e.tensor_reduce(
            c.amax[:, half:half + 1], c.qscr[:, 0:768], AX.X,
            ALU.max, apply_absolute_value=True), reads=[ld, c.ve])
    steps.append(lambda P: stepC1(P, 0))

    def stepC1b(P):
        stepC1(P, 1)
        P.op("vector", lambda e: e.tensor_tensor(
            c.amax[:, 0:1], c.amax[:, 0:1], c.amax[:, 1:2], ALU.max),
            reads=[c.ve])
    steps.append(stepC1b)

    def stepC2(P):
        emit_absmax_global(P, c, c.amax[:, 0:1], 3)
        recip127_row(P, c, 3, 4)
        P.op("vector", lambda e: e.tensor_scalar(
            c.rowtmp[0:1, 5:6], c.rowtmp[0:1, 3:4],
            float(S_CONV / S_XDBL / 127.0), None, ALU.mult), reads=[c.ve])
        emit_colscale(P, c, c.rowtmp[0:1, 4:5], 2, 0)
        emit_colscale(P, c, c.rowtmp[0:1, 5:6], 3, 1)
        emit_colguard(P, c, 3)
    steps.append(stepC2)

    def quantC(P, half, n, ld_fn, recip_col, codes_f, parts):
        ld = ld_fn(P, half)
        P.op("scalar", lambda e, n=n, parts=parts: e.activation(
            c.qscr[0:parts, 0:n], c.qscr[0:parts, 0:n], AF.Identity,
            bias=c.cmag[0:parts, 0:1], scale=recip_col),
            reads=[ld, c.sc, c.ve])
        P.op("vector", lambda e, n=n, parts=parts: e.tensor_scalar(
            c.qscr[0:parts, 0:n], c.qscr[0:parts, 0:n], CM, 127.0,
            ALU.subtract, ALU.min), reads=[c.sc, c.ve])
        P.op("vector", lambda e, half=half, n=n, parts=parts: e.tensor_scalar(
            codes_f[:, half * n:half * n + n], c.qscr[0:parts, 0:n],
            -128.0, None, ALU.max), reads=[c.ve])
    for half in (0, 1):
        steps.append(lambda P, half=half: quantC(
            P, half, 768, ld_wx, c.acol[:, 2:3], wxq_f, 128))

    def stepC4(P, half):
        ld = ld_wdt(P, half)
        P.op("vector", lambda e, half=half: e.tensor_reduce(
            c.amax[0:64, half:half + 1], c.qscr[0:64, 0:1024], AX.X,
            ALU.max, apply_absolute_value=True), reads=[ld, c.ve])
    steps.append(lambda P: stepC4(P, 0))

    def stepC4b(P):
        stepC4(P, 1)
        P.op("vector", lambda e: e.tensor_tensor(
            c.amax[0:64, 0:1], c.amax[0:64, 0:1], c.amax[0:64, 1:2],
            ALU.max), reads=[c.ve])
        P.op("vector", lambda e: e.memset(c.amax[64:128, 0:1], 0.0),
             reads=[c.ve])
    steps.append(stepC4b)

    def stepC5(P):
        emit_absmax_global(P, c, c.amax[:, 0:1], 6)
        recip127_row(P, c, 6, 7)
        P.op("vector", lambda e: e.tensor_scalar(
            c.rowtmp[0:1, 8:9], c.rowtmp[0:1, 6:7], float(1.0 / 127.0),
            None, ALU.mult), reads=[c.ve])
        emit_colscale(P, c, c.rowtmp[0:1, 7:8], 4, 0)
        emit_colscale(P, c, c.rowtmp[0:1, 8:9], 5, 1)
        emit_colguard(P, c, 5)
    steps.append(stepC5)
    for half in (0, 1):
        steps.append(lambda P, half=half: quantC(
            P, half, 1024, ld_wdt, c.acol[0:64, 4:5], wdtq_f, 64))

    def stepD1(P):
        P.op("vector", lambda e: e.tensor_reduce(
            c.amax[:, 0:1], cwbuf[:], AX.X, ALU.max,
            apply_absolute_value=True), reads=[c.q1, c.ve])
        emit_absmax_global(P, c, c.amax[:, 0:1], 9)
        recip127_row(P, c, 9, 10)
        P.op("vector", lambda e: e.tensor_scalar(
            c.rowtmp[0:1, 11:12], c.rowtmp[0:1, 9:10], float(S_XZ / 127.0),
            None, ALU.mult), reads=[c.ve])
        emit_colscale(P, c, c.rowtmp[0:1, 10:11], 6, 0)
        emit_colscale(P, c, c.rowtmp[0:1, 11:12], 7, 1)
        emit_colguard(P, c, 7)
    steps.append(stepD1)

    def stepD2(P):
        quant_weight2(P, c, cwbuf[:], NB_DI * DC, c.acol[:, 6:7],
                      c.qscr[:, 0:NB_DI * DC])
        P.op("vector", lambda e: e.tensor_scalar(
            c.taps[:], c.qscr[:, 0:NB_DI * DC], c.acol[:, 7:8],
            None, ALU.mult), reads=[c.ve, c.sc])
    steps.append(stepD2)
    steps.extend(stepsB)
    return steps


def phase_ffn1(nc, c, D, NT, subs_all, steps=()):
    with ExitStack() as stp:
        sbp = lambda n, s, d: stp.enter_context(nc.sbuf_tensor(n, s, d))
        psp = lambda n, s: stp.enter_context(nc.psum_tensor(n, s, F32))
        c.sq8 = sbp("sq8", [128, 2, 512], F32)
        c.xln = sbp("xln", [128, NB_DM, 1028], BF16)
        c.h_ffn = sbp("h_ffn", [128, NB_HF, 1028], BF16)
        c.wbufA = sbp("wbufA", [128, 2, DM], BF16)
        c.wbufB = sbp("wbufB", [128, 2, DM], BF16)
        c.wdbuf = sbp("wdbuf", [128, 2, HF], BF16)
        c.p_sum = psp("p_sum", [1, 512])
        c.p_sq = psp("p_sq", [1, 512])
        c.p_g = psp("p_g", [128, 2, 512])
        c.p_u = psp("p_u", [128, 2, 512])
        c.p_d = psp("p_d", [128, 2, 512])
        c.p_bc1 = c.p_g[:, 0, :]
        c.p_bc2 = c.p_g[:, 1, :]
        c.rA = Ring(2, c.rings[4:6])
        c.rB = Ring(2, c.rings[6:8])
        c.rD = Ring(2, c.rings[8:10])
        P = Prog(nc, c.esems, c.lw)
        c.resid_ready = (c.q1, c.q1.count)
        emit_ffn(P, c, split_supers(subs_all, 1028), D.wgq, D.wuq, D.wdq,
                 steps=steps)
        for b in range(NB_DM):
            P.op("sync", lambda e, b=b: e.dma_start(
                D.x1_dst(b), c.resid[:, b, HALO:NT]),
                reads=[c.resid_ready], dma=c.qst)
        P.drain(c.rings + [c.qg2])
        with nc.Block() as blk:
            P.realize(blk)


def mamba_in_alloc(nc, c, stp, M):
    sbp = lambda n, s, d: stp.enter_context(nc.sbuf_tensor(n, s, d))
    psp = lambda n, s: stp.enter_context(nc.psum_tensor(n, s, F32))
    c.sq8 = sbp("sq8b", [128, 2, 512], F32)
    M.u0s = sbp("u0s", [128, 2, NB_DM, 512], BF16)
    M.xroll = sbp("xroll", [128, NB_DI, 516], BF16)
    M.usub = sbp("usub", [128, NB_DI, 512], BF16)
    M.xdblc = sbp("xdblc", [96, 512], BF16)
    M.winb = sbp("winb", [128, 3, DM], BF16)
    M.zstg = sbp("zstg", [128, 2, 512], BF16)
    M.dstg = sbp("dstg", [128, 2, 512], BF16)
    M.convacc = sbp("convacc", [128, 2, 512], F32)
    c.p_sum = psp("p_sum2", [1, 512])
    c.p_sq = psp("p_sq2", [1, 512])
    M.p_w = psp("p_w", [128, 3, 512])
    c.p_bc1 = M.p_w[:, 0, :]
    c.p_bc2 = M.p_w[:, 1, :]
    M.p_x = psp("p_x", [96, 512])
    M.p_dt = psp("p_dt", [128, 2, 512])
    M.rWi = Ring(3, c.rings[0:3])
    M.rZ = Ring(2, c.swrings2[0:2])
    M.rDt = Ring(2, c.swrings2[2:4])
    M.rel_pw = [None, None, None]
    M.rel_pdt = [None, None]
    M.rel_cacc = [None, None]
    M.rel_ust = None
    M.rel_px = M.rel_usub = None
    M.rel_xdbl = []
    M.prev_fx = None
    M.first = True


def emit_mamba_subs(nc, c, D, P, M, sub_list, flush=True):
    u0s, xroll, usub, xdblc = M.u0s, M.xroll, M.usub, M.xdblc
    winb, zstg, dstg, convacc = M.winb, M.zstg, M.dstg, M.convacc
    p_w, p_x, p_dt = M.p_w, M.p_x, M.p_dt
    rWi, rZ, rDt = M.rWi, M.rZ, M.rDt
    if M.first:
        c.rel_sq = c.rel_ps = c.rel_bc = None
        c.rel_sq2 = [None, None]

    def conv_p1(b, w):
        js = b % 2
        P.op("vector", lambda e, b=b, js=js, w=w: e.tensor_scalar(
            convacc[:, js, 0:w], xroll[:, b, 1:1 + w],
            c.taps[:, b * DC:b * DC + 1], c.cb[:, b:b + 1],
            ALU.mult, ALU.add), reads=[M.rel_usub, M.rel_cacc[js], c.ve])
        for k in (1, 2, 3):
            P.op("vector", lambda e, b=b, k=k, js=js, w=w: e.scalar_tensor_tensor(
                convacc[:, js, 0:w], xroll[:, b, 1 + k:1 + k + w],
                c.taps[:, b * DC + k:b * DC + k + 1],
                convacc[:, js, 0:w], ALU.mult, ALU.add), reads=[c.ve])
        P.op("scalar", lambda e, js=js, w=w: e.activation(
            c.stage[:, 1024 + js * 512:1024 + js * 512 + w],
            convacc[:, js, 0:w], AF.Silu,
            bias=c.zero[:, 0:1]), reads=[c.ve, c.sc])
        M.rel_cacc[js] = (c.sc, P.es["scalar"].count)

    def conv_p2(b, w, own_off):
        js = b % 2
        P.op("vector", lambda e, js=js, w=w: e.tensor_scalar(
            c.stage[:, 2048 + js * 512:2048 + js * 512 + w],
            c.stage[:, 1024 + js * 512:1024 + js * 512 + w],
            1.0 / S_CONV, CM, ALU.mult, ALU.add), reads=[c.sc, c.ve])
        P.op("vector", lambda e, js=js, w=w: e.tensor_scalar(
            c.stage[:, 2048 + js * 512:2048 + js * 512 + w],
            c.stage[:, 2048 + js * 512:2048 + js * 512 + w],
            CM, 127.0, ALU.subtract, ALU.min), reads=[c.ve])
        P.op("vector", lambda e, b=b, js=js, w=w: e.tensor_scalar(
            usub[:, b, 0:w], c.stage[:, 2048 + js * 512:2048 + js * 512 + w],
            -128.0, None, ALU.max), reads=[c.ve, M.rel_ust])
        P.op("gpsimd", lambda e, b=b, w=w, oo=own_off: e.dma_start(
            D.u_dst(b, oo, w), usub[:, b, 0:w]),
            reads=[c.ve], dma=c.qust)

    for (si, (off, w)) in sub_list:
        own_off = off - HALO
        is_halo = (si == 0)
        j2 = si % 2
        pend_cv = None
        if not is_halo:
            pw_ = M.prev_fx
            P.op("vector", lambda e, pw_=pw_: e.tensor_copy(
                xroll[:, :, 0:4], xroll[:, :, pw_:pw_ + 4]), reads=[c.ve])

        def apply2(b, off, w):
            P.op("vector", lambda e, b=b, off=off, w=w: e.tensor_tensor(
                c.stage[:, 1024:1024 + w], c.resid[:, b, off:off + w],
                c.p_bc1[:, 0:w], ALU.mult), reads=[c.te, c.ve])
            P.op("vector", lambda e, w=w: e.tensor_tensor(
                c.stage[:, 1536:1536 + w], c.stage[:, 1024:1024 + w],
                c.p_bc2[:, 0:w], ALU.subtract), reads=[c.ve])
            P.op("vector", lambda e, w=w: e.tensor_scalar(
                c.stage[:, 2048:2048 + w], c.stage[:, 1536:1536 + w],
                CM, CM, ALU.add, ALU.subtract), reads=[c.ve])
            P.op("vector", lambda e, b=b, j2=j2, w=w: e.tensor_scalar(
                u0s[:, j2, b, 0:w], c.stage[:, 2048:2048 + w], 127.0,
                -128.0, ALU.min, ALU.max), reads=[c.ve])
        emit_ln(P, c, [(off, w)], apply2, fq_scale=1.0 / S_IN)
        u0_done = (c.ve, P.es["vector"].count)

        nwc = NB_DI if is_halo else NB_XZ
        for wc in range(nwc):
            jw = wc % 3
            j, rel = rWi.acquire()
            P.op("sync", lambda e, wc=wc, j=j: e.dma_start(
                winb[:, j, :], D.winq[wc]), reads=[rel], dma=rWi.sems[j])
            ld = (rWi.sems[j], rWi.sems[j].count)
            for dc in range(NB_DM):
                P.op("tensor", lambda e, dc=dc, j=j, j2=j2, w=w, jw=jw: e.matmul(
                    p_w[:, jw, 0:w], winb[:, j, dc * 128:(dc + 1) * 128],
                    u0s[:, j2, dc, 0:w], start=(dc == 0),
                    stop=(dc == NB_DM - 1)), reads=[ld, u0_done, M.rel_pw[jw]])
            rWi.release(j, (c.te, P.es["tensor"].count))
            if wc < NB_DI:
                M.rel_pw[jw] = emit_fq3(
                    P, c, xroll[:, wc, 4:4 + w],
                    p_w[:, jw, 0:w], w, c.acol[:, 1:2])
                if not is_halo:
                    conv_p1(wc, w)
                    if pend_cv is not None:
                        conv_p2(pend_cv, w, own_off)
                    pend_cv = wc
            else:
                if pend_cv is not None:
                    conv_p2(pend_cv, w, own_off)
                    pend_cv = None
                jz, relz = rZ.acquire()
                M.rel_pw[jw] = emit_fq3(
                    P, c, zstg[:, jz, 0:w], p_w[:, jw, 0:w],
                    w, c.acol[:, 1:2], extra_reads=[relz])
                P.op("gpsimd", lambda e, wc=wc, jz=jz, w=w, oo=own_off: e.dma_start(
                    D.z_dst(wc - NB_DI, oo, w), zstg[:, jz, 0:w]),
                    reads=[c.ve], dma=rZ.sems[jz])
                rZ.release(jz, (rZ.sems[jz], rZ.sems[jz].count))
        M.prev_fx = w
        if is_halo:
            continue
        if pend_cv is not None:
            conv_p2(pend_cv, w, own_off)
            pend_cv = None
        M.rel_ust = (c.qust, c.qust.count)
        u_done = (c.ve, P.es["vector"].count)

        for dc in range(NB_DI):
            P.op("tensor", lambda e, dc=dc, w=w: e.matmul(
                p_x[:, 0:w], c.wxq[:, dc, :], usub[:, dc, 0:w],
                start=(dc == 0), stop=(dc == NB_DI - 1)),
                reads=[u_done, M.rel_px])
        M.rel_usub = (c.te, P.es["tensor"].count)
        M.rel_px = emit_fq3(P, c, xdblc[:, 0:w], p_x[:, 0:w], w,
                            c.acol[0:96, 3:4], parts=96,
                            extra_reads=M.rel_xdbl)
        xdbl_done = (c.ve, P.es["vector"].count)
        P.op("gpsimd", lambda e, w=w, oo=own_off: e.dma_start(
            D.B_dst(oo, w), xdblc[64:80, 0:w]),
            reads=[xdbl_done], dma=c.qbc)
        P.op("gpsimd", lambda e, w=w, oo=own_off: e.dma_start(
            D.C_dst(oo, w), xdblc[80:96, 0:w]),
            reads=[xdbl_done], dma=c.qbc)
        bc_rel = (c.qbc, c.qbc.count)
        banks = [(p_dt[:, 0, :], ("pdt", 0)), (p_dt[:, 1, :], ("pdt", 1)),
                 (p_w[:, 0, :], ("pw", 0)), (p_w[:, 1, :], ("pw", 1)),
                 (p_w[:, 2, :], ("pw", 2))]
        for dc in range(NB_DI):
            bank, (kind, kk) = banks[dc % 5]
            rel = M.rel_pdt[kk] if kind == "pdt" else M.rel_pw[kk]
            P.op("tensor", lambda e, dc=dc, bank=bank, w=w: e.matmul(
                bank[:, 0:w], c.wdtq[:, dc, :], xdblc[0:64, 0:w],
                start=True, stop=True), reads=[xdbl_done, rel])
            jd, reld = rDt.acquire()
            newrel = emit_fq3(P, c, dstg[:, jd, 0:w],
                              bank[:, 0:w], w, c.acol[:, 5:6],
                              extra_reads=[reld])
            if kind == "pdt":
                M.rel_pdt[kk] = newrel
            else:
                M.rel_pw[kk] = newrel
            P.op("gpsimd", lambda e, dc=dc, jd=jd, w=w, oo=own_off: e.dma_start(
                D.dtq_dst(dc, oo, w), dstg[:, jd, 0:w]),
                reads=[c.ve], dma=rDt.sems[jd])
            rDt.release(jd, (rDt.sems[jd], rDt.sems[jd].count))
        M.rel_xdbl = [(c.te, P.es["tensor"].count), bc_rel]
    M.first = False


def phase_mamba_in(nc, c, D, NT, subs_all):
    with ExitStack() as stp:
        M = Ctx()
        mamba_in_alloc(nc, c, stp, M)
        P = Prog(nc, c.esems, c.lw)
        emit_mamba_subs(nc, c, D, P, M, list(enumerate(subs_all)))
        P.drain([c.qst, c.qstg, c.qust, c.qbc] + c.rings + c.swrings2)
        with nc.Block() as blk:
            P.realize(blk)


def phase_scan(nc, c, D, nb_lane, seq_len):
    TC = SC_TC
    NCH = seq_len // TC
    with ExitStack() as stp:
        sb = lambda n, s, d: stp.enter_context(nc.sbuf_tensor(n, s, d))
        abuf = sb("abuf", [128, 2, DS, TC], F32)
        hbuf = sb("hbuf", [128, DS, TC], BF16)
        bbuf = sb("bbuf", [128, DS, TC], BF16)
        yv = sb("yv", [128, DS, TC], BF16)
        brep = sb("brep", [128, 2, DS, TC], BF16)
        crep = sb("crep", [128, 2, DS, TC], BF16)
        ubuf = sb("ubuf", [128, 2, TC], BF16)
        dtbuf = sb("dtbuf", [128, 2, TC], BF16)
        dt32 = sb("dt32", [128, 2, TC], F32)
        spa = sb("spa", [128, 2, TC], F32)
        spb = sb("spb", [128, 2, TC], F32)
        spc = sb("spc", [128, TC], F32)
        wbf = sb("wbf", [128, TC], BF16)
        ystg = sb("ystg", [128, 4, TC], F32)
        hcar = sb("hcar", [128, nb_lane, DS], F32)
        alog = sb("alog", [128, nb_lane * DS], F32)
        Acol = sb("Acol", [128, nb_lane * DS], F32)
        dtb = sb("dtb", [128, nb_lane], F32)
        dD4 = sb("dD4", [128, nb_lane], F32)
        onec = sb("onec", [128, 1], F32)
        zeroc = sb("zeroc", [128, 1], F32)
        sc, ve, gp = c.sc, c.ve, c.esems["gpsimd"]
        q1 = c.q1
        rsB = c.rings[0:2]
        rsC = c.rings[2:4]
        rsU = c.rings[4:6]
        rsD = c.rings[6:8]
        rsY = c.swrings + c.swrings2[0:2]
        relB = [None, None]
        relC = [None, None]
        relU = [None, None]
        relD = [None, None]
        relSP = [None, None]
        pend_store = [None]
        relDT = [None, None]
        relA = [None, None]
        P = Prog(nc, c.esems, c.lw)

        P.op("vector", lambda e: e.memset(hcar[:], 0.0))
        P.op("vector", lambda e: e.memset(onec[:], 1.0))
        P.op("vector", lambda e: e.memset(zeroc[:], 0.0))
        P.op("sync", lambda e: e.dma_start(alog[:], D.al), dma=q1)
        P.op("sync", lambda e: e.dma_start(dtb[:], D.dtb), dma=q1)
        P.op("sync", lambda e: e.dma_start(dD4[:], D.dD), dma=q1)
        P.op("scalar", lambda e: e.activation(
            Acol[:], alog[:], AF.Exp, bias=zeroc[:, 0:1]), reads=[q1, ve, sc])
        P.op("vector", lambda e: e.tensor_scalar(
            Acol[:], Acol[:], -1.0, None, ALU.mult), reads=[sc, ve])
        P.op("vector", lambda e: e.tensor_scalar(
            dD4[:], dD4[:], float(S_CONV), None, ALU.mult), reads=[q1, ve])
        acol_done = (c.ve, P.es["vector"].count)

        for ch in range(NCH):
            t0 = ch * TC
            jB = ch % 2
            P.op("sync", lambda e, t0=t0, jB=jB: e.dma_start(
                brep[:, jB], D.b_src(t0).unsqueeze(0).broadcast_to(
                    [128, DS, TC])), reads=[relB[jB]], dma=rsB[jB])
            ldB = (rsB[jB], rsB[jB].count)
            P.op("sync", lambda e, t0=t0, jB=jB: e.dma_start(
                crep[:, jB], D.c_src(t0).unsqueeze(0).broadcast_to(
                    [128, DS, TC])), reads=[relC[jB]], dma=rsC[jB])
            ldC = (rsC[jB], rsC[jB].count)
            for db in range(nb_lane):
                it = ch * nb_lane + db
                js = it % 2
                jU = js
                jY = it % 4
                P.op("scalar", lambda e, t0=t0, db=db, jU=jU: e.dma_start(
                    ubuf[:, jU], D.u_src(db, t0)), reads=[relU[jU]],
                    dma=rsU[jU])
                ldU = (rsU[jU], rsU[jU].count)
                P.op("scalar", lambda e, t0=t0, db=db, jU=jU: e.dma_start(
                    dtbuf[:, jU], D.dt_src(db, t0)), reads=[relD[jU]],
                    dma=rsD[jU])
                ldD = (rsD[jU], rsD[jU].count)
                P.op("scalar", lambda e, db=db, jU=jU, js=js: e.activation(
                    spa[:, js], dtbuf[:, jU], AF.Relu, bias=dtb[:, db:db + 1],
                    scale=float(S_DT)), reads=[ldD, q1, sc, relSP[js]])
                P.op("scalar", lambda e, db=db, jU=jU, js=js: e.activation(
                    spb[:, js], dtbuf[:, jU], AF.Abs, bias=dtb[:, db:db + 1],
                    scale=float(S_DT)), reads=[sc])
                relD[jU] = (sc, P.es["scalar"].count)
                P.op("scalar", lambda e, js=js: e.activation(
                    spc[:], spb[:, js], AF.Exp, scale=-1.0), reads=[sc])
                P.op("scalar", lambda e, js=js: e.activation(
                    spb[:, js], spc[:], AF.Ln, bias=onec[:, 0:1]), reads=[sc])
                sp_done = (sc, P.es["scalar"].count)
                P.op("gpsimd", lambda e, js=js: e.tensor_tensor(
                    dt32[:, js], spa[:, js], spb[:, js], ALU.add),
                    reads=[sp_done, gp, relDT[js]])
                dt_done = (gp, P.es["gpsimd"].count)
                relSP[js] = dt_done
                if pend_store[0] is not None:
                    pdb, pt0, pjY, pvc = pend_store[0]
                    P.op("gpsimd", lambda e, pdb=pdb, pt0=pt0, pjY=pjY: e.dma_start(
                        D.y_dst(pdb, pt0), ystg[:, pjY]),
                        reads=[(c.ve, pvc)], dma=rsY[pjY])
                    pend_store[0] = None
                for n in range(DS):
                    P.op("scalar", lambda e, db=db, n=n, js=js: e.activation(
                        abuf[:, js, n, :], dt32[:, js], AF.Exp,
                        scale=Acol[:, db * DS + n:db * DS + n + 1]),
                        reads=[dt_done, relA[js], sc, acol_done])
                a_done = (sc, P.es["scalar"].count)
                P.op("vector", lambda e, jU=jU, js=js: e.scalar_tensor_tensor(
                    wbf[:], ubuf[:, jU], float(S_CONV * S_XDBL), dt32[:, js],
                    ALU.mult, ALU.mult), reads=[ldU, dt_done, ve, acol_done])
                wdone = (c.ve, P.es["vector"].count)
                P.op("vector", lambda e, jB=jB: e.tensor_tensor(
                    bbuf[:], wbf[:].unsqueeze(1).broadcast_to([128, DS, TC]),
                    brep[:, jB], ALU.mult), reads=[ldB, ve])
                relB[jB] = (ve, P.es["vector"].count)
                for n in range(DS):
                    P.op("vector", lambda e, db=db, n=n, js=js: e.tensor_tensor_scan(
                        hbuf[:, n, :], abuf[:, js, n, :], bbuf[:, n, :],
                        hcar[:, db, n:n + 1], ALU.mult, ALU.add),
                        reads=[a_done, ve])
                scans_done = (c.ve, P.es["vector"].count)
                relA[js] = scans_done
                relDT[js] = (wdone, scans_done) if False else scans_done
                P.op("vector", lambda e, db=db: e.tensor_copy(
                    hcar[:, db, :], hbuf[:, :, TC - 1:TC]), reads=[ve])
                P.op("vector", lambda e, jB=jB: e.tensor_tensor(
                    yv[:], hbuf[:], crep[:, jB], ALU.mult), reads=[ldC, ve])
                relC[jB] = (ve, P.es["vector"].count)
                for hw in (8, 4, 2, 1):
                    P.op("vector", lambda e, hw=hw: e.tensor_tensor(
                        yv[:, 0:hw, :], yv[:, 0:hw, :],
                        yv[:, hw:2 * hw, :], ALU.add), reads=[ve])
                P.op("vector", lambda e, jY=jY: e.tensor_scalar(
                    ystg[:, jY], yv[:, 0, :], float(S_XDBL), None,
                    ALU.mult), reads=[ve, (rsY[jY], rsY[jY].count)])
                P.op("vector", lambda e, db=db, jU=jU, jY=jY: e.scalar_tensor_tensor(
                    ystg[:, jY], ubuf[:, jU], dD4[:, db:db + 1], ystg[:, jY],
                    ALU.mult, ALU.add), reads=[ve])
                relU[jU] = (ve, P.es["vector"].count)
                pend_store[0] = (db, t0, jY, P.es["vector"].count)
        pdb, pt0, pjY, pvc = pend_store[0]
        P.op("gpsimd", lambda e, pdb=pdb, pt0=pt0, pjY=pjY: e.dma_start(
            D.y_dst(pdb, pt0), ystg[:, pjY]), reads=[(c.ve, pvc)], dma=rsY[pjY])
        P.drain(rsY + [rsB[0], rsB[1], rsC[0], rsC[1],
                 rsU[0], rsU[1], rsD[0], rsD[1]])
        with nc.Block() as blk:
            P.realize(blk)


def phase_wout_prep(nc, c, D, whq, h128b, h128f, load_consts):
    with ExitStack() as stp:
        sbp = lambda n, s, d: stp.enter_context(nc.sbuf_tensor(n, s, d))
        psp = lambda n, s: stp.enter_context(nc.psum_tensor(n, s, F32))
        s1 = sbp("s1", [128, NB_DI, DM], F32)
        wob = sbp("wob", [128, 2, DM], F32)
        tmpb = sbp("tmpb", [128, DM], F32)
        c.p_col = psp("p_col3", [128, 2])
        p_wo = psp("p_wo", [128, DM])
        P = Prog(nc, c.esems, c.lw)
        if load_consts:
            P.op("vector", lambda e: e.memset(c.cmag[:], CM))
            P.op("vector", lambda e: e.memset(c.zero[:], 0.0))
            P.op("vector", lambda e: e.memset(c.eps[:], EPS))
            P.op("sync", lambda e: e.dma_start(c.ones_c[:], D.oc), dma=c.q1)
            P.op("sync", lambda e: e.dma_start(c.ones_r[:], D.orr), dma=c.q1)
            for (src, dst) in ((D.wg, D.wgq), (D.wu, D.wuq), (D.wd, D.wdq)):
                P.op("gpsimd", lambda e, s=src, d=dst: e.dma_start(d, s),
                     dma=c.qg)
        P.op("vector", lambda e: e.memset(c.amax[:, 0:1], 0.0), reads=[c.ve])
        for b in range(NB_DM):
            P.op("sync", lambda e, b=b: e.dma_start(
                c.resid[:, b, :], D.x1_src(b)), dma=c.q1)
        P.op("sync", lambda e: e.dma_start(h128b[:], D.h128b), dma=c.q1)
        P.op("sync", lambda e: e.dma_start(h128f[:], D.h128f), dma=c.q1)
        rW = Ring(2, c.rings[0:2])
        rel_pwo = None
        for b in range(NB_DI):
            j, rel = rW.acquire()
            P.op("sync", lambda e, b=b, j=j: e.dma_start(
                wob[:, j, :], D.wo[b]), reads=[rel], dma=rW.sems[j])
            ld = (rW.sems[j], rW.sems[j].count)
            for hh in range(2):
                P.op("tensor", lambda e, j=j, hh=hh: e.matmul(
                    p_wo[:, hh * 512:(hh + 1) * 512], h128f[:],
                    wob[:, j, hh * 512:(hh + 1) * 512],
                    start=True, stop=True), reads=[ld, c.q1, rel_pwo])
            rW.release(j, (c.te, P.es["tensor"].count))
            P.op("scalar", lambda e, b=b: e.copy(s1[:, b, :], p_wo[:]),
                 reads=[c.te, c.sc])
            rel_pwo = (c.sc, P.es["scalar"].count)
        s1_done = rel_pwo
        for lvl in (8, 4, 2, 1):
            for g0 in range(0, NB_DI, 2 * lvl):
                for i in range(lvl):
                    A_, B_ = g0 + i, g0 + i + lvl
                    P.op("vector", lambda e, A_=A_, B_=B_: e.tensor_tensor(
                        tmpb[:], s1[:, A_, :], s1[:, B_, :], ALU.subtract),
                        reads=[s1_done, c.ve])
                    P.op("vector", lambda e, A_=A_, B_=B_: e.tensor_tensor(
                        s1[:, A_, :], s1[:, A_, :], s1[:, B_, :], ALU.add),
                        reads=[c.ve])
                    P.op("vector", lambda e, B_=B_: e.tensor_copy(
                        s1[:, B_, :], tmpb[:]), reads=[c.ve])
        for b in range(NB_DI):
            P.op("vector", lambda e, b=b: e.tensor_reduce(
                c.amax[:, 1:2], s1[:, b, :], AX.X, ALU.max,
                apply_absolute_value=True), reads=[c.ve])
            P.op("vector", lambda e: e.tensor_tensor(
                c.amax[:, 0:1], c.amax[:, 0:1], c.amax[:, 1:2], ALU.max),
                reads=[c.ve])
        emit_absmax_global(P, c, c.amax[:, 0:1], 0)
        recip127_row(P, c, 0, 1)
        P.op("vector", lambda e: e.tensor_scalar(
            c.rowtmp[0:1, 2:3], c.rowtmp[0:1, 0:1],
            float(S_Y / 127.0 / np.sqrt(DI)), None, ALU.mult), reads=[c.ve])
        emit_colscale(P, c, c.rowtmp[0:1, 1:2], 0, 0)
        emit_colscale(P, c, c.rowtmp[0:1, 2:3], 1, 1)
        emit_colguard(P, c, 1)
        for b in range(NB_DI):
            quant_weight(P, c, s1[:, b, :], DM, c.acol[:, 0:1], whq[:, b, :])
        P.drain(c.rings + [c.q1, c.qg2, c.qg])
        with nc.Block() as blk:
            P.realize(blk)


def phase_mix(nc, c, D, NT, whq, h128b):
    with ExitStack() as stp:
        sbp = lambda n, s, d: stp.enter_context(nc.sbuf_tensor(n, s, d))
        psp = lambda n, s: stp.enter_context(nc.psum_tensor(n, s, F32))
        ybuf = sbp("ybuf", [128, NB_DI, 256], F32)
        zbuf = sbp("zbuf", [128, NB_DI, 256], BF16)
        yq = sbp("yq", [128, NB_DI, 256], BF16)
        f1 = sbp("f1", [128, NB_DI, 256], F32)
        f2 = sbp("f2", [128, NB_DI, 256], F32)
        yh = sbp("yh", [128, NB_DI, 256], BF16)
        p_all = psp("p_all", [128, NB_DI, 256])
        rY = Ring(1, c.rings[2:3])
        rZ = Ring(1, c.rings[4:5])
        P = Prog(nc, c.esems, c.lw)
        st = Ctx()
        st.rel_yq = st.rel_f = st.rel_po = None
        tiles = [(off, min(256, NT - off)) for off in range(0, NT, 256)]

        def pre(t):
            off, w = tiles[t]
            jy, rely = rY.acquire()
            P.op("sync", lambda e, jy=jy, off=off, w=w: e.dma_start(
                ybuf[:, :, 0:w], D.y_src(off, w)), reads=[rely],
                dma=rY.sems[jy])
            st.ldY = (rY.sems[jy], rY.sems[jy].count)
            jz, relz = rZ.acquire()
            P.op("scalar", lambda e, jz=jz, off=off, w=w: e.dma_start(
                zbuf[:, :, 0:w], D.z_src(off, w)), reads=[relz],
                dma=rZ.sems[jz])
            st.ldZ = (rZ.sems[jz], rZ.sems[jz].count)
            st.jy, st.jz = jy, jz

        def fqblock(t):
            off, w = tiles[t]
            ldY, ldZ = st.ldY, st.ldZ
            for b in range(NB_DI):
                P.op("scalar", lambda e, b=b, w=w: e.activation(
                    c.stage[:, 0:w], zbuf[:, b, 0:w], AF.Silu,
                    scale=float(S_XZ), bias=c.zero[:, 0:1]),
                    reads=[ldZ, c.sc, c.ve])
                P.op("vector", lambda e, b=b, w=w: e.tensor_tensor(
                    c.stage[:, 512:512 + w], ybuf[:, b, 0:w],
                    c.stage[:, 0:w], ALU.mult),
                    reads=[ldY, c.sc, c.ve, st.rel_yq])
                P.op("vector", lambda e, w=w: e.tensor_scalar(
                    c.stage[:, 1024:1024 + w], c.stage[:, 512:512 + w],
                    float(1.0 / S_Y), CM, ALU.mult, ALU.add), reads=[c.ve])
                P.op("vector", lambda e, w=w: e.tensor_scalar(
                    c.stage[:, 1536:1536 + w], c.stage[:, 1024:1024 + w],
                    CM, 127.0, ALU.subtract, ALU.min), reads=[c.ve])
                P.op("vector", lambda e, b=b, w=w: e.tensor_scalar(
                    yq[:, b, 0:w], c.stage[:, 1536:1536 + w], -128.0,
                    None, ALU.max), reads=[c.ve])
            done = (c.ve, P.es["vector"].count)
            rY.release(st.jy, done)
            rZ.release(st.jz, done)
            st.yq_done = done

        def s1(t):
            off, w = tiles[t]
            for b in range(NB_DI):
                P.op("tensor", lambda e, b=b, w=w: e.matmul(
                    p_all[:, b, 0:w], h128b[:], yq[:, b, 0:w],
                    start=True, stop=True), reads=[st.yq_done, c.q1, st.rel_f])
            st.rel_yq = (c.te, P.es["tensor"].count)
            for b in range(NB_DI):
                P.op("scalar", lambda e, b=b, w=w: e.copy(
                    f1[:, b, 0:w], p_all[:, b, 0:w]),
                    reads=[c.te, c.sc, c.ve])
            st.copies = (c.sc, P.es["scalar"].count)

        def btf_yh(t):
            off, w = tiles[t]
            P.waits("vector", [st.copies])
            cur, nxt = f1, f2
            for lvl in (8, 4, 2, 1):
                for g0 in range(0, NB_DI, 2 * lvl):
                    for i in range(lvl):
                        A_, B_ = g0 + i, g0 + i + lvl
                        P.op("vector", lambda e, A_=A_, B_=B_, cur=cur, nxt=nxt, w=w: e.tensor_tensor(
                            nxt[:, A_, 0:w], cur[:, A_, 0:w],
                            cur[:, B_, 0:w], ALU.add), reads=[c.ve])
                        P.op("vector", lambda e, A_=A_, B_=B_, cur=cur, nxt=nxt, w=w: e.tensor_tensor(
                            nxt[:, B_, 0:w], cur[:, A_, 0:w],
                            cur[:, B_, 0:w], ALU.subtract), reads=[c.ve])
                cur, nxt = nxt, cur
            fw_done = (c.ve, P.es["vector"].count)
            for b in range(NB_DI):
                P.op("scalar", lambda e, b=b, cur=cur, w=w: e.activation(
                    c.stage[:, 2048:2048 + w], cur[:, b, 0:w], AF.Identity,
                    scale=float(1.0 / np.sqrt(DI)), bias=c.cmag[:, 0:1]),
                    reads=[fw_done, c.sc, c.ve, c.te])
                P.op("vector", lambda e, w=w: e.tensor_scalar(
                    c.stage[:, 2560:2560 + w], c.stage[:, 2048:2048 + w], CM,
                    127.0, ALU.subtract, ALU.min), reads=[c.sc, c.ve])
                P.op("vector", lambda e, b=b, w=w: e.tensor_scalar(
                    yh[:, b, 0:w], c.stage[:, 2560:2560 + w], -128.0,
                    None, ALU.max), reads=[c.ve])
            st.yh_done = (c.ve, P.es["vector"].count)

        def wout(t):
            off, w = tiles[t]
            for oc in range(NB_DM):
                for b in range(NB_DI):
                    P.op("tensor", lambda e, oc=oc, b=b, w=w: e.matmul(
                        p_all[:, 0, 0:w], whq[:, b, oc * 128:(oc + 1) * 128],
                        yh[:, b, 0:w], start=(b == 0),
                        stop=(b == NB_DI - 1)), reads=[st.yh_done, st.rel_po])
                P.op("scalar", lambda e, w=w: e.activation(
                    c.stage[:, 3072:3072 + w], p_all[:, 0, 0:w],
                    AF.Identity, scale=c.acol[:, 1:2], bias=c.zero[:, 0:1]),
                    reads=[c.te, c.sc])
                P.op("vector", lambda e, oc=oc, off=off, w=w: e.tensor_tensor(
                    c.resid[:, oc, off:off + w],
                    c.resid[:, oc, off:off + w],
                    c.stage[:, 3072:3072 + w], ALU.add), reads=[c.sc, c.ve])
                st.rel_po = (c.ve, P.es["vector"].count)
            st.rel_f = (c.sc, P.es["scalar"].count)

        pre(0)
        fqblock(0)
        s1(0)
        for t in range(len(tiles)):
            btf_yh(t)
            if t + 1 < len(tiles):
                pre(t + 1)
                fqblock(t + 1)
            wout(t)
            if t + 1 < len(tiles):
                s1(t + 1)
        P.drain(c.rings)
        with nc.Block() as blk:
            P.realize(blk)


def phase_ffn2(nc, c, D, NT):
    with ExitStack() as stp:
        sbp = lambda n, s, d: stp.enter_context(nc.sbuf_tensor(n, s, d))
        psp = lambda n, s: stp.enter_context(nc.psum_tensor(n, s, F32))
        c.sq8 = sbp("sq83", [128, 2, 512], F32)
        c.xln = sbp("xln3", [128, NB_DM, 1028], BF16)
        c.h_ffn = sbp("h_ffn3", [128, NB_HF, 1028], BF16)
        c.wbufA = sbp("wbufA3", [128, 2, DM], BF16)
        c.wbufB = sbp("wbufB3", [128, 2, DM], BF16)
        c.wdbuf = sbp("wdbuf3", [128, 2, HF], BF16)
        c.p_sum = psp("p_sum3", [1, 512])
        c.p_sq = psp("p_sq3", [1, 512])
        c.p_g = psp("p_g3", [128, 2, 512])
        c.p_u = psp("p_u3", [128, 2, 512])
        c.p_d = psp("p_d3", [128, 2, 512])
        c.p_bc1 = c.p_g[:, 0, :]
        c.p_bc2 = c.p_g[:, 1, :]
        c.rA = Ring(2, c.rings[6:8])
        c.rB = Ring(2, c.rings[8:10])
        c.rD = Ring(2, c.rings[10:12])
        P = Prog(nc, c.esems, c.lw)
        c.resid_ready = (c.ve, c.esems["vector"].count)
        c.rel_sq = c.rel_ps = c.rel_bc = None
        c.rel_sq2 = [None, None]
        subs = subtiles(NT)
        emit_ffn(P, c, split_supers(subs, 1028), D.wgq, D.wuq, D.wdq)
        for b in range(NB_DM):
            P.op("sync", lambda e, b=b: e.dma_start(
                D.out[bass.ts(b, 128), :], c.resid[:, b, :]),
                reads=[c.resid_ready], dma=c.qst)
        P.waits("sync", [c.qst])
        with nc.Block() as blk:
            P.realize(blk)


def emit_collective(nc, c, kind, in_ap, out_ap, pre_waits, ncores=NCORES):
    P = Prog(nc, c.esems, c.lw)
    P.waits("gpsimd", pre_waits)

    def build(e):
        with allow_ici_all_to_all():
            return e.collective_compute(
                kind, ALU.bypass, replica_groups=[list(range(ncores))],
                ins=[in_ap.opt()], outs=[out_ap.opt()])
    P.op("gpsimd", build)
    done = (c.esems["gpsimd"], c.esems["gpsimd"].count)
    with nc.Block() as blk:
        P.realize(blk)
    return done


def _l1_dram(nc, nt_own):
    NT = nt_own + HALO
    D = D_()
    D.x = nc.dram_tensor("x_fm", [DM, NT], F32, kind="ExternalInput").ap()
    D.win = nc.dram_tensor("win_sw", [NB_XZ, 128, DM], F32, kind="ExternalInput").ap()
    D.wx = nc.dram_tensor("wx_sw", [NB_DI, 128, 96], F32, kind="ExternalInput").ap()
    D.wdt = nc.dram_tensor("wdt_sw", [NB_DI, 64, 128], F32, kind="ExternalInput").ap()
    D.wg = nc.dram_tensor("wg_sw", [NB_HF, 128, DM], I32, kind="ExternalInput").ap()
    D.wu = nc.dram_tensor("wu_sw", [NB_HF, 128, DM], I32, kind="ExternalInput").ap()
    D.wd = nc.dram_tensor("wd_sw", [NB_DM, 128, HF], I32, kind="ExternalInput").ap()
    D.cw = nc.dram_tensor("cw_fm", [128, NB_DI * DC], F32, kind="ExternalInput").ap()
    D.cbn = nc.dram_tensor("cb_fm", [128, NB_DI], F32, kind="ExternalInput").ap()
    D.oc = nc.dram_tensor("ones_col", [128, 1], F32, kind="ExternalInput").ap()
    D.orr = nc.dram_tensor("ones_row", [1, 128], F32, kind="ExternalInput").ap()
    D.wgq = nc.dram_tensor("wg_q", [NB_HF, 128, DM], BF16).ap()
    D.wuq = nc.dram_tensor("wu_q", [NB_HF, 128, DM], BF16).ap()
    D.wdq = nc.dram_tensor("wd_q", [NB_DM, 128, HF], BF16).ap()
    D.winq = nc.dram_tensor("win_q", [NB_XZ, 128, DM], BF16).ap()
    return D


def _alloc_stage(nc, st, c, tag=""):
    c.stage = st.enter_context(nc.sbuf_tensor("stage" + tag, [128, 3584], F32))
    c.lnrow = st.enter_context(nc.sbuf_tensor("lnrow" + tag, [1, 2048], F32))


def _alloc_l1_persist(nc, st, c, NT):
    c.resid = st.enter_context(nc.sbuf_tensor("resid", [128, NB_DM, NT], F32))
    c.wxq = st.enter_context(nc.sbuf_tensor("wxq", [128, NB_DI, 96], BF16))
    c.wdtq = st.enter_context(nc.sbuf_tensor("wdtq", [64, NB_DI, 128], BF16))
    c.taps = st.enter_context(nc.sbuf_tensor("taps", [128, NB_DI * DC], F32))
    c.cb = st.enter_context(nc.sbuf_tensor("cb", [128, NB_DI], F32))


def build_l1(nt_own):
    NT = nt_own + HALO
    nc = bass.Bass("TRN2", target_bir_lowering=False)
    D = _l1_dram(nc, nt_own)
    d_x1 = nc.dram_tensor("x1_fm", [DM, nt_own], F32, kind="ExternalOutput").ap()
    d_z = nc.dram_tensor("z_codes", [DI, nt_own], BF16, kind="ExternalOutput").ap()
    d_u = nc.dram_tensor("u_codes", [DI, nt_own], BF16, kind="ExternalOutput").ap()
    d_dtq = nc.dram_tensor("dtq_codes", [DI, nt_own], BF16, kind="ExternalOutput").ap()
    d_B = nc.dram_tensor("B_codes", [DS, nt_own], BF16, kind="ExternalOutput").ap()
    d_C = nc.dram_tensor("C_codes", [DS, nt_own], BF16, kind="ExternalOutput").ap()
    D.x1_dst = lambda b: d_x1[bass.ts(b, 128), :]
    D.z_dst = lambda blk, oo, w: d_z[bass.ts(blk, 128), oo:oo + w]
    D.u_dst = lambda b, oo, w: d_u[bass.ts(b, 128), oo:oo + w]
    D.dtq_dst = lambda dc, oo, w: d_dtq[bass.ts(dc, 128), oo:oo + w]
    D.B_dst = lambda oo, w: d_B[:, oo:oo + w]
    D.C_dst = lambda oo, w: d_C[:, oo:oo + w]
    subs_all = subtiles(NT, first=HALO)
    with ExitStack() as st0:
        c = new_ctx(nc, st0)
        _alloc_stage(nc, st0, c)
        _alloc_l1_persist(nc, st0, c, NT)
        make_sems(nc, st0, c)
        with ExitStack() as stq:
            p0_alloc(nc, stq, c)
            phase_p0(nc, c, D)
            phase_ffn1(nc, c, D, NT, subs_all,
                       steps=p0_quant_steps(nc, c, D))
        phase_mamba_in(nc, c, D, NT, subs_all)
    return nc


def build_l2(seq_len, nb_lane):
    nc = bass.Bass("TRN2", target_bir_lowering=False)
    LD = nb_lane * 128
    d_u = nc.dram_tensor("u_lane", [LD, seq_len], BF16, kind="ExternalInput").ap()
    d_dt = nc.dram_tensor("dtq_lane", [LD, seq_len], BF16, kind="ExternalInput").ap()
    d_b = nc.dram_tensor("bcode", [DS, seq_len], BF16, kind="ExternalInput").ap()
    d_cc = nc.dram_tensor("ccode", [DS, seq_len], BF16, kind="ExternalInput").ap()
    D = D_()
    D.al = nc.dram_tensor("alog_fm", [128, nb_lane * DS], F32, kind="ExternalInput").ap()
    D.dtb = nc.dram_tensor("dtb_fm", [128, nb_lane], F32, kind="ExternalInput").ap()
    D.dD = nc.dram_tensor("dD_fm", [128, nb_lane], F32, kind="ExternalInput").ap()
    d_y = nc.dram_tensor("y_lane", [LD, seq_len], F32, kind="ExternalOutput").ap()
    TC = SC_TC
    D.u_src = lambda db, t0: d_u[bass.ts(db, 128), t0:t0 + TC]
    D.dt_src = lambda db, t0: d_dt[bass.ts(db, 128), t0:t0 + TC]
    D.b_src = lambda t0: d_b[:, t0:t0 + TC]
    D.c_src = lambda t0: d_cc[:, t0:t0 + TC]
    D.y_dst = lambda db, t0: d_y[bass.ts(db, 128), t0:t0 + TC]
    with ExitStack() as st0:
        c = Ctx()
        make_sems(nc, st0, c)
        phase_scan(nc, c, D, nb_lane, seq_len)
    return nc


def _l3_dram(nc, nt_own):
    D = D_()
    D.wo = nc.dram_tensor("wo_sw", [NB_DI, 128, DM], F32, kind="ExternalInput").ap()
    D.h128b = nc.dram_tensor("h128b", [128, 128], BF16, kind="ExternalInput").ap()
    D.h128f = nc.dram_tensor("h128f", [128, 128], F32, kind="ExternalInput").ap()
    D.out = nc.dram_tensor("out_fm", [DM, nt_own], F32, kind="ExternalOutput").ap()
    return D


def build_l3(nt_own):
    NT = nt_own
    nc = bass.Bass("TRN2", target_bir_lowering=False)
    D = _l3_dram(nc, nt_own)
    d_x1 = nc.dram_tensor("x1_fm", [DM, NT], F32, kind="ExternalInput").ap()
    d_y = nc.dram_tensor("y_tok", [DI, NT], F32, kind="ExternalInput").ap()
    d_z = nc.dram_tensor("z_codes", [DI, NT], BF16, kind="ExternalInput").ap()
    D.wg = nc.dram_tensor("wg_sw", [NB_HF, 128, DM], I32, kind="ExternalInput").ap()
    D.wu = nc.dram_tensor("wu_sw", [NB_HF, 128, DM], I32, kind="ExternalInput").ap()
    D.wd = nc.dram_tensor("wd_sw", [NB_DM, 128, HF], I32, kind="ExternalInput").ap()
    D.oc = nc.dram_tensor("ones_col", [128, 1], F32, kind="ExternalInput").ap()
    D.orr = nc.dram_tensor("ones_row", [1, 128], F32, kind="ExternalInput").ap()
    D.wgq = nc.dram_tensor("wg_q3", [NB_HF, 128, DM], BF16).ap()
    D.wuq = nc.dram_tensor("wu_q3", [NB_HF, 128, DM], BF16).ap()
    D.wdq = nc.dram_tensor("wd_q3", [NB_DM, 128, HF], BF16).ap()
    D.x1_src = lambda b: d_x1[bass.ts(b, 128), :]
    D.y_src = lambda off, w: d_y.rearrange("(b p) t -> p b t", p=128)[:, :, off:off + w]
    D.z_src = lambda off, w: d_z.rearrange("(b p) t -> p b t", p=128)[:, :, off:off + w]
    with ExitStack() as st0:
        c = new_ctx(nc, st0)
        _alloc_stage(nc, st0, c)
        c.resid = st0.enter_context(
            nc.sbuf_tensor("resid3", [128, NB_DM, NT], F32))
        h128b = st0.enter_context(nc.sbuf_tensor("h128bs", [128, 128], BF16))
        h128f = st0.enter_context(nc.sbuf_tensor("h128fs", [128, 128], F32))
        make_sems(nc, st0, c)
        with ExitStack() as stq:
            whq = stq.enter_context(
                nc.sbuf_tensor("whq", [128, NB_DI, DM], BF16))
            phase_wout_prep(nc, c, D, whq, h128b, h128f, load_consts=True)
            phase_mix(nc, c, D, NT, whq, h128b)
        phase_ffn2(nc, c, D, NT)
    return nc


def build_fused(nt_own, ncores):
    NT = nt_own + HALO
    nb_lane = NB_DI // ncores
    LD = nb_lane * 128
    seq_len = nt_own * ncores
    nc = bass.Bass("TRN2", target_bir_lowering=False)
    D = _l1_dram(nc, nt_own)
    D3 = _l3_dram(nc, nt_own)
    D3.wgq, D3.wuq, D3.wdq = D.wgq, D.wuq, D.wdq
    D2 = D_()
    D2.al = nc.dram_tensor("alog_fm", [128, nb_lane * DS], F32, kind="ExternalInput").ap()
    D2.dtb = nc.dram_tensor("dtb_fm", [128, nb_lane], F32, kind="ExternalInput").ap()
    D2.dD = nc.dram_tensor("dD_fm", [128, nb_lane], F32, kind="ExternalInput").ap()
    r_spill = nc.dram_tensor("r_spill", [DM, nt_own], F32).ap()
    z_scr = nc.dram_tensor("z_scr", [DI, nt_own], BF16).ap()
    cc1_in = nc.dram_tensor("cc1_in", [ncores, 2, LD, nt_own], BF16).ap()
    cc1_out = nc.dram_tensor("cc1_out", [ncores, 2, LD, nt_own], BF16).ap()
    ccg_in = nc.dram_tensor("ccg_in", [2, DS, nt_own], BF16).ap()
    ccg_out = nc.dram_tensor("ccg_out", [ncores, 2, DS, nt_own], BF16).ap()
    cc2_in = nc.dram_tensor("cc2_in", [ncores, nb_lane, 128, nt_own], F32).ap()
    cc2_out = nc.dram_tensor("cc2_out", [ncores, nb_lane, 128, nt_own], F32).ap()

    D.x1_dst = lambda b: r_spill[bass.ts(b, 128), :]
    D.z_dst = lambda blk, oo, w: z_scr[bass.ts(blk, 128), oo:oo + w]
    D.u_dst = lambda b, oo, w: cc1_in[b // nb_lane, 0,
                                      (b % nb_lane) * 128:(b % nb_lane) * 128 + 128,
                                      oo:oo + w]
    D.dtq_dst = lambda dc, oo, w: cc1_in[dc // nb_lane, 1,
                                         (dc % nb_lane) * 128:(dc % nb_lane) * 128 + 128,
                                         oo:oo + w]
    D.B_dst = lambda oo, w: ccg_in[0, :, oo:oo + w]
    D.C_dst = lambda oo, w: ccg_in[1, :, oo:oo + w]
    TC = SC_TC
    D2.u_src = lambda db, t0: cc1_out[t0 // nt_own, 0,
                                      db * 128:db * 128 + 128,
                                      t0 % nt_own:t0 % nt_own + TC]
    D2.dt_src = lambda db, t0: cc1_out[t0 // nt_own, 1,
                                       db * 128:db * 128 + 128,
                                       t0 % nt_own:t0 % nt_own + TC]
    D2.b_src = lambda t0: ccg_out[t0 // nt_own, 0, :,
                                  t0 % nt_own:t0 % nt_own + TC]
    D2.c_src = lambda t0: ccg_out[t0 // nt_own, 1, :,
                                  t0 % nt_own:t0 % nt_own + TC]
    D2.y_dst = lambda db, t0: cc2_in[t0 // nt_own, db, :,
                                     t0 % nt_own:t0 % nt_own + TC]
    D3.x1_src = lambda b: r_spill[bass.ts(b, 128), :]
    D3.y_src = lambda off, w: cc2_out.rearrange(
        "k l p t -> p (k l) t")[:, :, off:off + w]
    D3.z_src = lambda off, w: z_scr.rearrange(
        "(b p) t -> p b t", p=128)[:, :, off:off + w]

    subs_all = subtiles(NT, first=HALO)
    with ExitStack() as st0:
        c = new_ctx(nc, st0)
        make_sems(nc, st0, c)
        with ExitStack() as stA:
            _alloc_stage(nc, stA, c, "A")
            _alloc_l1_persist(nc, stA, c, NT)
            with ExitStack() as stq:
                p0_alloc(nc, stq, c)
                phase_p0(nc, c, D)
                phase_ffn1(nc, c, D, NT, subs_all,
                           steps=p0_quant_steps(nc, c, D))
            phase_mamba_in(nc, c, D, NT, subs_all)
        cc1_done = emit_collective(nc, c, "AllToAll", cc1_in, cc1_out,
                                   [c.qust, c.swrings2[2], c.swrings2[3]],
                                   ncores)
        ccg_done = emit_collective(nc, c, "AllGather", ccg_in, ccg_out,
                                   [c.qbc], ncores)
        with ExitStack() as stB:
            Pw = Prog(nc, c.esems, c.lw)
            Pw.waits("sync", [cc1_done, ccg_done])
            Pw.waits("scalar", [cc1_done, ccg_done])
            with nc.Block() as blk:
                Pw.realize(blk)
            phase_scan(nc, c, D2, nb_lane, seq_len)
        cc2_done = emit_collective(nc, c, "AllToAll", cc2_in, cc2_out,
                                   c.swrings + c.swrings2[0:2], ncores)
        with ExitStack() as stC:
            _alloc_stage(nc, stC, c, "C")
            c.resid = stC.enter_context(
                nc.sbuf_tensor("resid3", [128, NB_DM, nt_own], F32))
            h128b = stC.enter_context(nc.sbuf_tensor("h128bs", [128, 128], BF16))
            h128f = stC.enter_context(nc.sbuf_tensor("h128fs", [128, 128], F32))
            with ExitStack() as stq:
                whq = stq.enter_context(
                    nc.sbuf_tensor("whq", [128, NB_DI, DM], BF16))
                phase_wout_prep(nc, c, D3, whq, h128b, h128f,
                                load_consts=False)
                Pw = Prog(nc, c.esems, c.lw)
                Pw.waits("sync", [cc2_done])
                Pw.waits("scalar", [cc2_done])
                with nc.Block() as blk:
                    Pw.realize(blk)
                phase_mix(nc, c, D3, nt_own, whq, h128b)
            phase_ffn2(nc, c, D3, nt_own)
    return nc


def stage_weights(z):
    f32 = np.float32
    wgT = np.ascontiguousarray(z['L0_ffn_w_gate'].T)
    wuT = np.ascontiguousarray(z['L0_ffn_w_up'].T)
    wdT = np.ascontiguousarray(z['L0_ffn_w_down'].T)
    wg_sw = np.ascontiguousarray(
        wgT.reshape(NB_DM, 128, NB_HF, 128).transpose(2, 1, 0, 3).reshape(NB_HF, 128, DM))
    wu_sw = np.ascontiguousarray(
        wuT.reshape(NB_DM, 128, NB_HF, 128).transpose(2, 1, 0, 3).reshape(NB_HF, 128, DM))
    wd_sw = np.ascontiguousarray(
        wdT.reshape(NB_HF, 128, NB_DM, 128).transpose(2, 1, 0, 3).reshape(NB_DM, 128, HF))
    winT = np.ascontiguousarray(z['L0_attn_W_in'].T.astype(f32))
    win_sw = np.ascontiguousarray(
        winT.reshape(NB_DM, 128, NB_XZ, 128).transpose(2, 1, 0, 3).reshape(NB_XZ, 128, DM))
    wx_sw = np.ascontiguousarray(z['L0_attn_W_x'].T.astype(f32).reshape(NB_DI, 128, 96))
    wdt_sw = np.ascontiguousarray(
        z['L0_attn_W_dt'].T.astype(f32).reshape(64, NB_DI, 128).transpose(1, 0, 2))
    cw = z['L0_attn_conv_w'].astype(f32)
    cw_fm = np.ascontiguousarray(
        cw.reshape(NB_DI, 128, DC).transpose(1, 0, 2).reshape(128, NB_DI * DC))
    cb_fm = np.ascontiguousarray(z['L0_attn_conv_b'].astype(f32).reshape(NB_DI, 128).T)
    wo_sw = np.ascontiguousarray(z['L0_attn_W_out'].T.astype(f32).reshape(NB_DI, 128, DM))
    i = np.arange(128)
    H128 = np.where((np.bitwise_count(i[:, None] & i[None, :]) % 2) == 0,
                    1.0, -1.0).astype(f32)
    return dict(wg_sw=wg_sw, wu_sw=wu_sw, wd_sw=wd_sw, win_sw=win_sw,
                wx_sw=wx_sw, wdt_sw=wdt_sw, cw_fm=cw_fm, cb_fm=cb_fm,
                wo_sw=wo_sw, h128f=H128, h128b=H128.astype(BF),
                ones_col=np.ones((128, 1), f32), ones_row=np.ones((1, 128), f32))


def _lane_params(z, j, nbl):
    f32 = np.float32
    sl = slice(j * nbl * 128, (j + 1) * nbl * 128)
    al = z['L0_attn_A_log'].astype(f32)[sl].reshape(nbl, 128, DS)
    return dict(
        alog_fm=np.ascontiguousarray(al.transpose(1, 0, 2).reshape(128, nbl * DS)),
        dtb_fm=np.ascontiguousarray(
            z['L0_attn_dt_bias'].astype(f32)[sl].reshape(nbl, 128).T),
        dD_fm=np.ascontiguousarray(
            z['L0_attn_D'].astype(f32)[sl].reshape(nbl, 128).T))


def stage_l1(x_tokens, W, ncores, nt_own):
    maps = []
    for k in range(ncores):
        lo = k * nt_own - HALO
        xs = np.zeros((nt_own + HALO, DM), np.float32)
        a = max(0, -lo)
        xs[a:] = x_tokens[lo + a:lo + nt_own + HALO]
        maps.append(dict(
            x_fm=np.ascontiguousarray(xs.T),
            win_sw=W['win_sw'], wx_sw=W['wx_sw'], wdt_sw=W['wdt_sw'],
            wg_sw=W['wg_sw'], wu_sw=W['wu_sw'], wd_sw=W['wd_sw'],
            cw_fm=W['cw_fm'], cb_fm=W['cb_fm'],
            ones_col=W['ones_col'], ones_row=W['ones_row']))
    return maps


def stage_l2(r1, z, ncores, nt_own):
    u_all = np.concatenate([np.asarray(r['u_codes']) for r in r1], axis=1)
    dt_all = np.concatenate([np.asarray(r['dtq_codes']) for r in r1], axis=1)
    B_all = np.concatenate([np.asarray(r['B_codes']) for r in r1], axis=1)
    C_all = np.concatenate([np.asarray(r['C_codes']) for r in r1], axis=1)
    nbl = NB_DI // ncores
    ld = nbl * 128
    maps = []
    for j in range(ncores):
        sl = slice(j * ld, (j + 1) * ld)
        m = dict(
            u_lane=np.ascontiguousarray(u_all[sl]),
            dtq_lane=np.ascontiguousarray(dt_all[sl]),
            bcode=np.ascontiguousarray(B_all),
            ccode=np.ascontiguousarray(C_all))
        m.update(_lane_params(z, j, nbl))
        maps.append(m)
    return maps


def stage_l3(r1, r2, W, ncores, nt_own):
    y_all = np.concatenate([np.asarray(r['y_lane']) for r in r2], axis=0)
    maps = []
    for k in range(ncores):
        sl = slice(k * nt_own, (k + 1) * nt_own)
        maps.append(dict(
            x1_fm=np.asarray(r1[k]['x1_fm']),
            y_tok=np.ascontiguousarray(y_all[:, sl]),
            z_codes=np.asarray(r1[k]['z_codes']),
            wg_sw=W['wg_sw'], wu_sw=W['wu_sw'], wd_sw=W['wd_sw'],
            wo_sw=W['wo_sw'], h128b=W['h128b'], h128f=W['h128f'],
            ones_col=W['ones_col'], ones_row=W['ones_row']))
    return maps


def stage_fused(x_tokens, W, z, ncores, nt_own):
    maps = stage_l1(x_tokens, W, ncores, nt_own)
    nbl = NB_DI // ncores
    for k in range(ncores):
        maps[k].update(_lane_params(z, k, nbl))
        maps[k].update(wo_sw=W['wo_sw'], h128b=W['h128b'], h128f=W['h128f'])
    return maps


def kernel(**inputs):
    z = inputs
    x = np.asarray(z['L0_attn_hidden_states'][0], np.float32)
    W = stage_weights(z)
    cores = list(range(NCORES))
    if USE_FUSED:
        nc = build_fused(NT_OWN, NCORES)
        r = run_bass_kernel_spmd(nc, stage_fused(x, W, z, NCORES, NT_OWN),
                                 core_ids=cores).results
        out = np.concatenate([np.asarray(rr['out_fm']).T for rr in r], axis=0)
        return out[None].astype(np.float32)
    l1 = build_l1(NT_OWN)
    r1 = run_bass_kernel_spmd(l1, stage_l1(x, W, NCORES, NT_OWN),
                              core_ids=cores).results
    l2 = build_l2(SEQ, NB_DI // NCORES)
    r2 = run_bass_kernel_spmd(l2, stage_l2(r1, z, NCORES, NT_OWN),
                              core_ids=cores).results
    l3 = build_l3(NT_OWN)
    r3 = run_bass_kernel_spmd(l3, stage_l3(r1, r2, W, NCORES, NT_OWN),
                              core_ids=cores).results
    out = np.concatenate([np.asarray(r['out_fm']).T for r in r3], axis=0)
    return out[None].astype(np.float32)
